# version 03
# baseline (speedup 1.0000x reference)
import contextlib
import numpy as np
import concourse.bass as bass
import concourse.mybir as mybir
from concourse.bass_utils import run_bass_kernel_spmd

F32 = mybir.dt.float32
BF16 = mybir.dt.bfloat16
F16 = mybir.dt.float16
U32 = mybir.dt.uint32
I32 = mybir.dt.int32
ALU = mybir.AluOpType
ACTF = mybir.ActivationFunctionType
AX = mybir.AxisListType.X

MAGIC = float(np.float32(1.5 * 2 ** 23))
EPS = 1e-5
QMAX = 127.0
EXP_BIAS = -8.0

DD = 1024
DC = 8
ECK = 4
TCH = 32
KC = 32
QG = 8
II = 2816
ICN = 22
TOWN = 2048

ENGINES = ("tensor", "scalar", "vector", "gpsimd", "sync")


class Sem:
    def __init__(self, handle):
        self.h = handle
        self.n = 0

    def now(self):
        return self.n


class Prog:
    def __init__(self, nc):
        self.nc = nc
        self.ev = {e: [] for e in ENGINES}

    def emit(self, eng, fn, wait=(), inc=()):
        self.ev[eng].append((fn, tuple(wait), tuple(inc)))
        for s, n in inc:
            s.n += n

    def replay(self):
        nc = self.nc
        with nc.Block() as block:
            for name in ENGINES:
                evs = self.ev[name]

                def make(evs):
                    def run(eng):
                        last = {}
                        for fn, waits, incs in evs:
                            for s, v in waits:
                                if v > 0 and last.get(id(s), -1) < v:
                                    eng.wait_ge(s.h, v)
                                    last[id(s)] = v
                            inst = fn(eng)
                            for s, n in incs:
                                inst = inst.then_inc(s.h, n)
                    return run

                getattr(block, name)(make(evs))


PREPASS_EARLY = True
BARRIER_LEVEL = 3



def build_program(taps=()):
    nc = bass.Bass("TRN2", target_bir_lowering=False)
    taps = set(taps)

    d_xtm = nc.dram_tensor("x_tm", [4096, DD], F32, kind="ExternalInput").ap()
    d_xown = nc.dram_tensor("x_own_fm", [DD, TOWN], F32, kind="ExternalInput").ap()
    d_wq = nc.dram_tensor("wq_b", [128, 4096], F32, kind="ExternalInput").ap()
    d_wk = nc.dram_tensor("wk_b", [128, 4096], F32, kind="ExternalInput").ap()
    d_wv = nc.dram_tensor("wv_b", [128, 4096], F32, kind="ExternalInput").ap()
    d_wo = nc.dram_tensor("wo_b", [128, 8192], F32, kind="ExternalInput").ap()
    d_wgn = nc.dram_tensor("wg_nat", [II, DD], F32, kind="ExternalInput").ap()
    d_wun = nc.dram_tensor("wu_nat", [II, DD], F32, kind="ExternalInput").ap()
    d_wdn = nc.dram_tensor("wd_nat", [DD, II], F32, kind="ExternalInput").ap()
    d_wgt = nc.dram_tensor("wg_tb", [II, DD], F32, kind="ExternalInput").ap()
    d_wut = nc.dram_tensor("wu_tb", [II, DD], F32, kind="ExternalInput").ap()
    d_wdt = nc.dram_tensor("wd_tb", [DD, II], F32, kind="ExternalInput").ap()
    d_bout = nc.dram_tensor("b_out_pm", [128, 8], F32, kind="ExternalInput").ap()
    d_clip = nc.dram_tensor("clip2", [1, 2], F32, kind="ExternalInput").ap()
    d_psel = nc.dram_tensor("psel", [1, 1], U32, kind="ExternalInput").ap()
    d_out = nc.dram_tensor("out_fm", [DD, TOWN], F32, kind="ExternalOutput").ap()

    d_arx_i = nc.dram_tensor("arx_i", [1, 16], F32)
    d_arx_o = nc.dram_tensor("arx_o", [1, 16], F32)
    d_arh_i = nc.dram_tensor("arh_i", [1, 16], F32)
    d_arh_o = nc.dram_tensor("arh_o", [1, 16], F32)
    d_ag_i = nc.dram_tensor("ag_i", [128, 8192], F16)
    d_ag_o = nc.dram_tensor("ag_o", [256, 8192], F16)

    tap_d = {}
    for t in taps:
        shp = {"xn": [128, 16384], "xnp": [128, 16384], "kf": [128, 16384],
               "vt": [128, 16384], "ao": [128, 16384], "r": [128, 16384],
               "m": [128, 45056], "xq": [128, 16384], "q0": [128, 2048],
               "aop": [128, 8192], "sc": [128, 64]}[t]
        tap_d[t] = nc.dram_tensor("tap_" + t, shp, F32, kind="ExternalOutput").ap()

    A = nc.alloc_sbuf_tensor
    arena_xn = A("arena_xn", [128, 16384], F16)
    arena_m = A("arena_m", [128, 45056], BF16)
    arena_v = A("arena_v", [128, 16384], BF16)
    arena_ao = A("arena_ao", [128, 16384], BF16)
    wq_sb = A("wq_sb", [128, 4096], F16)

    scr32 = A("scr32", [128, 512], F32)
    junkb = scr32.bitcast(BF16)
    xntm = A("xntm", [128, 1024], F16)
    smcols = A("smcols", [128, 32], F32)
    sqcols = A("sqcols", [128, 32], F32)
    wrk = A("wrk", [128, 64], F32)
    rstd = A("rstd", [128, 32], F32)
    nmrc = A("nmrc", [128, 32], F32)
    segred = A("segred", [128, 144], F32)
    xmax = A("xmax", [128, 32], F32)
    mwide = segred
    selA = A("selA", [128, 128], F32)
    selB = A("selB", [128, 128], F32)
    ident = A("ident", [128, 128], F16)
    identi = A("identi", [128, 128], I32)
    onesb = A("onesb", [128, 128], BF16)
    ones1f = A("ones1f", [1, 128], F32)
    pmc = A("pmc", [128, 4], F32)
    bout_f = A("bout_f", [128, 8], F32)
    clip_sb = A("clip_sb", [1, 2], F32)
    sig_sb = A("sig_sb", [1, 2], F32)
    scal = A("scal", [1, 16], F32)
    lmax_sb = A("lmax_sb", [1, 144], F32)
    gmax_sb = A("gmax_sb", [1, 16], F32)
    hmax_sb = A("hmax_sb", [1, 144], F32)
    ghmax_sb = A("ghmax_sb", [1, 16], F32)
    mk_pm = A("mk_pm", [128, 1], F32)
    lnmk_pm = A("lnmk_pm", [128, 1], F32)
    halfc = A("halfc", [128, 1], F32)
    rsx_pm = A("rsx_pm", [128, 1], F32)
    sx_pm = A("sx_pm", [128, 1], F32)
    rsh_pm = A("rsh_pm", [128, 1], F32)
    sh_pm = A("sh_pm", [128, 1], F32)
    rmg = A("rmg", [128, 22], F32)
    rmu = A("rmu", [128, 22], F32)
    rmd = A("rmd", [128, 8], F32)
    rgc = A("rgc", [128, 22], F32)
    ruc = A("ruc", [128, 22], F32)
    rdc = A("rdc", [128, 8], F32)
    sgx = A("sgx", [128, 22], F32)
    sux = A("sux", [128, 22], F32)
    sdx = A("sdx", [128, 8], F32)
    rstg = A("rstg", [1, 256], F32)
    rstgd = A("rstgd", [1, 128], F32)
    psel_sb = A("psel_sb", [1, 1], U32)
    redt = A("redt", [128, 4], F32)
    bcq = A("bcq", [128, 256], F32)
    sg_sb = A("sg_sb", [128, 512], F16)
    ep_sb = A("ep_sb", [128, 1024], F16)

    am = arena_m
    am32 = arena_m.bitcast(F32)
    am16 = arena_m.bitcast(F16)
    ao = arena_ao
    ao32 = arena_ao.bitcast(F32)
    ao16 = arena_ao.bitcast(F16)
    vt = arena_v
    r_r = arena_v.bitcast(F16)
    xn_own = arena_xn
    axn32 = arena_xn.bitcast(F32)

    XNP = 0
    KF = 16384
    PT = 32768
    QOD = 36864
    SPARE = 40960
    RSUM32 = SPARE // 2
    AOEV = SPARE + 1024
    QAUG = SPARE + 2048
    NATF = (SPARE + 3072) // 2
    WKA = PT
    WVA = QOD
    WO = 0
    AOPART = 8192

    WGST = 0
    WUST = 2048
    WGI = 8192
    WUI = 12288
    XJIT = 0
    OUTSB = 1024
    WDI = 4096

    ps_av = nc.alloc_psum_tensor("ps_av", [128, 1024], F32)
    ps_sum = nc.alloc_psum_tensor("ps_sum", [128, 512], F32)
    ps_b3 = nc.alloc_psum_tensor("ps_b3", [128, 512], F32)
    ps_sc = nc.alloc_psum_tensor("ps_sc", [128, 2048], F32)
    ps_scb = ps_sc.bitcast(BF16)
    ps_sc16 = ps_sc.bitcast(F16)

    p = Prog(nc)
    marks = {}
    nc._sem_stack = contextlib.ExitStack()
    sems = {}

    def sem(name):
        if name not in sems:
            sems[name] = Sem(nc._sem_stack.enter_context(nc.semaphore(name)))
        return sems[name]

    s_ld = sem("ld")
    s_x = sem("xload")
    s_init = sem("init")
    s_stat = sem("stat")
    s_ntm = sem("ntm")
    s_act = sem("actn")
    s_tr = sem("tr")
    s_xev = sem("xev")
    s_kv = sem("kvmm")
    s_kvev = sem("kvev")
    s_qmm = sem("qmm")
    s_qev = sem("qev")
    s_scm = sem("scm")
    s_exp = sem("exp")
    s_avm = sem("avm")
    s_summ = sem("summ")
    s_nrm = sem("nrm")
    s_bc = sem("bc")
    s_cc = sem("cc")
    s_cdma = sem("cdma")
    s_sx = sem("sx")
    s_sh = sem("sh")
    s_wl = sem("wl")
    s_nat = sem("nat")
    s_scl = sem("scl")
    s_wq = sem("wqz")
    s_fmm = sem("fmm")
    s_fep = sem("fep")
    s_mq = sem("mq")
    s_xq = sem("xq")
    s_op = sem("opm")
    s_rep = sem("rep")
    s_st = sem("st")
    s_tap = sem("tapd")
    s_aug = sem("aug")
    s_kn = sem("kn")
    s_bar = sem("bar")
    barn = [0]

    def barrier():
        barn[0] += 1
        tgt = 5 * barn[0]
        for eng in ENGINES:
            if eng in ("tensor", "vector", "scalar"):
                p.emit(eng, lambda e: e.drain(), inc=[(s_bar, 1)])
            else:
                p.emit(eng, lambda e: e.nop(), inc=[(s_bar, 1)])
        for eng in ENGINES:
            p.emit(eng, lambda e, tgt=tgt: e.nop(), wait=[(s_bar, tgt)])

    def init_phase():
        g = "gpsimd"
        p.emit(g, lambda e: e.dma_start(out=clip_sb[:, :], in_=d_clip), inc=[(s_ld, 16)])
        p.emit(g, lambda e: e.dma_start(out=bout_f[:, :], in_=d_bout), inc=[(s_ld, 16)])
        p.emit(g, lambda e: e.dma_start(out=psel_sb[:, :], in_=d_psel), inc=[(s_ld, 16)])
        p.emit(g, lambda e: e.memset(onesb[:, :], 1.0))
        p.emit(g, lambda e: e.memset(ones1f[:, :], 1.0))
        p.emit(g, lambda e: e.memset(pmc[:, 0:1], 0.125))
        p.emit(g, lambda e: e.memset(pmc[:, 1:2], 0.0))
        p.emit(g, lambda e: e.memset(pmc[:, 2:3], EPS))
        p.emit(g, lambda e: e.memset(halfc[:, :], 0.5))
        p.emit(g, lambda e: e.memset(selA[:, :], 0.0))
        p.emit(g, lambda e: e.memset(selB[:, :], 0.0))
        p.emit(g, lambda e: e.memset(selA[0:1, 0:64], 1.0))
        p.emit(g, lambda e: e.memset(selA[32:33, 64:128], 1.0))
        p.emit(g, lambda e: e.memset(selB[64:65, 0:64], 1.0))
        p.emit(g, lambda e: e.memset(selB[96:97, 64:128], 1.0))
        p.emit(g, lambda e: e.iota(identi[:, :], [[1, 128]], channel_multiplier=-1,
                                   allow_small_or_imprecise_dtypes=True),
               inc=[(s_init, 1)])
        p.emit("vector", lambda e: e.tensor_single_scalar(ident[:, :], identi[:, :],
                                                          0.0, ALU.is_equal),
               wait=[(s_init, 1)], inc=[(s_init, 1)])
        p.emit(g, lambda e: e.dma_start(out=wq_sb[:, :], in_=d_wq), inc=[(s_ld, 16)])
        p.emit(g, lambda e: e.dma_start(out=am16[:, WKA:WKA + 4096], in_=d_wk),
               inc=[(s_ld, 16)])
        p.emit(g, lambda e: e.dma_start(out=am16[:, WVA:WVA + 4096], in_=d_wv),
               inc=[(s_ld, 16)])
        p.emit("scalar", lambda e: e.activation(sig_sb[:, :], clip_sb[:, :], ACTF.Sigmoid),
               wait=[(s_ld, 16 * 3)], inc=[(s_sx, 1)])
        p.emit("scalar", lambda e: e.activation(scal[0:1, 14:15], clip_sb[0:1, 0:1],
                                                ACTF.Sqrt))
        p.emit("scalar", lambda e: e.activation(scal[0:1, 14:15], clip_sb[0:1, 0:1],
                                                ACTF.Exp))
        p.emit("scalar", lambda e: e.activation(scal[0:1, 14:15], clip_sb[0:1, 0:1],
                                                ACTF.Silu))
        p.emit("scalar", lambda e: e.drain(), inc=[(s_init, 1)])

    def phase_A():
        for g in range(4):
            for i in range(8):
                tc = 8 * g + i
                p.emit("sync", lambda e, tc=tc, i=i: e.dma_start(
                    out=ao32[:, i * 1024:(i + 1) * 1024],
                    in_=d_xtm[tc * 128:(tc + 1) * 128, :]),
                    wait=([(s_ntm, 8 * (g - 1) + i + 1)] if g >= 1 else []),
                    inc=[(s_x, 16)])
                p.emit("vector", lambda e, i=i, tc=tc: e.tensor_scalar(
                    junkb[:, :], ao32[:, i * 1024:(i + 1) * 1024], 0.0, 0.0,
                    ALU.add, ALU.add, accum_out=smcols[:, tc:tc + 1]),
                    wait=[(s_x, s_x.now())])
                p.emit("vector", lambda e, i=i, tc=tc: e.scalar_tensor_tensor(
                    junkb[:, :], ao32[:, i * 1024:(i + 1) * 1024], 1.0,
                    ao32[:, i * 1024:(i + 1) * 1024], ALU.mult, ALU.mult,
                    accum_out=sqcols[:, tc:tc + 1]))
            c0 = 8 * g
            V = "vector"
            p.emit(V, lambda e: e.drain())
            p.emit(V, lambda e, c0=c0: e.tensor_scalar(
                wrk[:, 0:8], smcols[:, c0:c0 + 8], 1.0 / DD, 0.0, ALU.mult, ALU.add))
            p.emit(V, lambda e, c0=c0: e.tensor_scalar(
                wrk[:, 8:16], sqcols[:, c0:c0 + 8], 1.0 / DD, 0.0, ALU.mult, ALU.add))
            p.emit(V, lambda e: e.drain())
            p.emit(V, lambda e: e.tensor_tensor(wrk[:, 16:24], wrk[:, 0:8],
                                                wrk[:, 0:8], ALU.mult))
            p.emit(V, lambda e: e.drain())
            p.emit(V, lambda e: e.tensor_tensor(wrk[:, 24:32], wrk[:, 8:16],
                                                wrk[:, 16:24], ALU.subtract),
                   inc=[(s_stat, 1)])
            p.emit("scalar", lambda e: e.activation(wrk[:, 32:40], wrk[:, 24:32],
                                                    ACTF.Sqrt, bias=pmc[:, 2:3]),
                   wait=[(s_stat, s_stat.now()), (s_init, 1)])
            p.emit("scalar", lambda e: e.drain(), inc=[(s_act, 1)])
            p.emit(V, lambda e: e.reciprocal(wrk[:, 40:48], wrk[:, 32:40]),
                   wait=[(s_act, s_act.now())])
            p.emit(V, lambda e: e.drain())
            p.emit(V, lambda e: e.tensor_tensor(wrk[:, 48:56], wrk[:, 32:40],
                                                wrk[:, 40:48], ALU.mult))
            p.emit(V, lambda e: e.drain())
            p.emit(V, lambda e: e.tensor_tensor(wrk[:, 56:64], wrk[:, 48:56],
                                                wrk[:, 40:48], ALU.mult))
            p.emit(V, lambda e: e.drain())
            p.emit(V, lambda e, c0=c0: e.scalar_tensor_tensor(
                rstd[:, c0:c0 + 8], wrk[:, 40:48], 2.0, wrk[:, 56:64],
                ALU.mult, ALU.subtract))
            p.emit(V, lambda e: e.drain())
            p.emit(V, lambda e, c0=c0: e.scalar_tensor_tensor(
                nmrc[:, c0:c0 + 8], wrk[:, 0:8], -1.0, rstd[:, c0:c0 + 8],
                ALU.mult, ALU.mult))
            p.emit(V, lambda e: e.drain())
            for i in range(8):
                tc = 8 * g + i
                p.emit(V, lambda e, i=i, tc=tc: e.tensor_scalar(
                    xntm[:, :], ao32[:, i * 1024:(i + 1) * 1024],
                    rstd[:, tc:tc + 1], nmrc[:, tc:tc + 1], ALU.mult, ALU.add),
                    wait=[(s_tr, 2 * tc)], inc=[(s_ntm, 1)])
                p.emit(V, lambda e, tc=tc: e.reduce_max(
                    xmax[:, tc:tc + 1], xntm[:, :], AX, apply_absolute_value=True),
                    inc=[(s_xev, 1)])
                for half in range(2):
                    for q in range(4):
                        dcq = half * 4 + q
                        p.emit("tensor", lambda e, dcq=dcq, half=half, q=q: e.transpose(
                            ps_sc16[:, half * 1024 + q * 256:half * 1024 + q * 256 + 128],
                            xntm[:, dcq * 128:(dcq + 1) * 128], ident[:, :]),
                            wait=([(s_ntm, s_ntm.now()), (s_xev, s_xev.now())]
                                  if q == 0 else []),
                            inc=([(s_tr, 1)] if q == 3 else []))
                    for q in range(4):
                        dcq = half * 4 + q
                        if tc < 16:
                            tgt, toff = xn_own, dcq * 2048 + tc * 128
                        else:
                            tgt, toff = am16, XNP + dcq * 2048 + (tc - 16) * 128
                        p.emit("vector", lambda e, tgt=tgt, toff=toff, half=half, q=q:
                               e.tensor_copy(tgt[:, toff:toff + 128],
                                             ps_sc16[:, half * 1024 + q * 256:half * 1024 + q * 256 + 128]),
                               wait=[(s_tr, s_tr.now())],
                               inc=([(s_xev, 1)] if q == 3 else []))
        xn_done = s_xev.now()

        kvps = [ps_av[:, 0:512], ps_av[:, 512:1024], ps_sum[:, :], ps_b3[:, :]]
        for ec in range(ECK):
            for tg in range(QG):
                bank = (ec * QG + tg) % 4
                for dc in range(DC):
                    def kmm(e, ec=ec, tg=tg, dc=dc, bank=bank):
                        if tg < 4:
                            rhs = xn_own[:, dc * 2048 + tg * 512:dc * 2048 + tg * 512 + 512]
                        else:
                            rhs = am16[:, XNP + dc * 2048 + (tg - 4) * 512:
                                       XNP + dc * 2048 + (tg - 4) * 512 + 512]
                        return e.matmul(kvps[bank],
                                        am16[:, WKA + dc * 512 + ec * 128:
                                           WKA + dc * 512 + (ec + 1) * 128],
                                        rhs, start=(dc == 0), stop=(dc == 7))
                    w = []
                    if ec == 0 and tg == 0 and dc == 0:
                        w = [(s_xev, xn_done), (s_ld, 16 * 6)]
                    elif dc == 0:
                        w = [(s_kvev, max(0, s_kvev.now() - 3))]
                    p.emit("tensor", kmm, wait=w, inc=([(s_kv, 1)] if dc == 7 else []))
                p.emit("vector", lambda e, ec=ec, tg=tg, bank=bank: e.tensor_copy(
                    am16[:, KF + ec * 4096 + tg * 512:KF + ec * 4096 + tg * 512 + 512],
                    kvps[bank]),
                    wait=[(s_kv, s_kv.now())], inc=[(s_kvev, 1)])
        for tc in range(TCH):
            bank = tc % 4
            for dc in range(DC):
                def vmm(e, tc=tc, dc=dc, bank=bank):
                    if tc < 16:
                        lhs = xn_own[:, dc * 2048 + tc * 128:dc * 2048 + (tc + 1) * 128]
                    else:
                        lhs = am16[:, XNP + dc * 2048 + (tc - 16) * 128:
                                   XNP + dc * 2048 + (tc - 15) * 128]
                    return e.matmul(kvps[bank], lhs,
                                    am16[:, WVA + dc * 512:WVA + (dc + 1) * 512],
                                    start=(dc == 0), stop=(dc == 7))
                p.emit("tensor", vmm,
                       wait=([(s_kvev, max(0, s_kvev.now() - 3))] if dc == 0 else []),
                       inc=([(s_kv, 1)] if dc == 7 else []))
            p.emit("vector", lambda e, tc=tc, bank=bank: e.tensor_copy(
                vt[:, tc * 512:(tc + 1) * 512], kvps[bank]),
                wait=[(s_kv, s_kv.now())], inc=[(s_kvev, 1)])
        kv_done = s_kvev.now()

        for ec in range(ECK):
            p.emit("vector", lambda e, ec=ec: e.tensor_tensor(
                ao[:, 0:4096], am16[:, KF + ec * 4096:KF + (ec + 1) * 4096],
                am16[:, KF + ec * 4096:KF + (ec + 1) * 4096], ALU.mult),
                wait=([(s_kvev, 32)] if ec == 0 else []), inc=[(s_kn, 1)])
            for tg in range(QG):
                p.emit("tensor", lambda e, tg=tg: e.matmul(
                    ps_b3[0:64, :], onesb[0:64, 0:64],
                    ao[0:64, tg * 512:(tg + 1) * 512],
                    start=True, stop=True, tile_position=(0, 0)),
                    wait=[(s_kn, s_kn.now()), (s_aug, s_aug.now()),
                          (s_kvev, s_kvev.now())])
                p.emit("tensor", lambda e, tg=tg: e.matmul(
                    ps_b3[64:128, :], onesb[64:128, 64:128],
                    ao[64:128, tg * 512:(tg + 1) * 512],
                    start=True, stop=True, tile_position=(64, 64)),
                    inc=[(s_kn, 1)])
                p.emit("vector", lambda e: e.reduce_max(
                    wrk[:, 9:10], ps_b3[:, :], AX), wait=[(s_kn, s_kn.now())])
                p.emit("vector", lambda e: e.drain())
                if ec == 0 and tg == 0:
                    p.emit("vector", lambda e: e.tensor_copy(wrk[:, 8:9], wrk[:, 9:10]),
                           inc=[(s_aug, 1)])
                else:
                    p.emit("vector", lambda e: e.tensor_max(wrk[:, 8:9], wrk[:, 8:9],
                                                            wrk[:, 9:10]),
                           inc=[(s_aug, 1)])
                p.emit("vector", lambda e: e.drain())
        p.emit("gpsimd", lambda e: e.dma_start(out=lmax_sb[0:1, 0:128], in_=wrk[:, 8:9],
                                               allow_slow_non_contiguous=True),
               wait=[(s_aug, s_aug.now())], inc=[(s_cdma, 16)])
        p.emit("vector", lambda e: e.reduce_max(wrk[0:1, 10:11], lmax_sb[0:1, 0:128], AX),
               wait=[(s_cdma, s_cdma.now())])
        p.emit("vector", lambda e: e.drain(), inc=[(s_aug, 1)])
        p.emit("scalar", lambda e: e.activation(wrk[0:1, 11:12], wrk[0:1, 10:11],
                                                ACTF.Sqrt),
               wait=[(s_aug, s_aug.now())])
        p.emit("scalar", lambda e: e.drain(), inc=[(s_aug, 1)])
        p.emit("vector", lambda e: e.tensor_scalar(wrk[0:1, 12:13], wrk[0:1, 11:12],
                                                   -1.0, 0.0, ALU.mult, ALU.add),
               wait=[(s_aug, s_aug.now())])
        p.emit("vector", lambda e: e.drain(), inc=[(s_aug, 1)])
        p.emit("tensor", lambda e: e.matmul(ps_b3[:, 0:1], ones1f[0:1, :],
                                            wrk[0:1, 12:13], start=True, stop=True),
               wait=[(s_aug, s_aug.now())], inc=[(s_bc, 1)])
        p.emit("vector", lambda e: e.tensor_copy(mk_pm[:, :], ps_b3[:, 0:1]),
               wait=[(s_bc, s_bc.now())])
        p.emit("vector", lambda e: e.drain(), inc=[(s_aug, 1)])
        p.emit("vector", lambda e: e.tensor_scalar(redt[:, 2:3], mk_pm[:, 0:1], -1.0,
                                                   0.0, ALU.mult, ALU.add),
               wait=[(s_aug, s_aug.now())])
        p.emit("vector", lambda e: e.drain(), inc=[(s_aug, 1)])
        p.emit("scalar", lambda e: e.activation(lnmk_pm[:, :], redt[:, 2:3], ACTF.Ln),
               wait=[(s_aug, s_aug.now())])
        p.emit("scalar", lambda e: e.drain(), inc=[(s_aug, 1)])
        kn_done = s_aug.now()

        barrier()
        p.emit("vector", lambda e: e.drain(), wait=[(s_xev, xn_done)])
        p.emit("vector", lambda e: e.reduce_max(redt[:, 0:1], xmax[:, :], AX,
                                                apply_absolute_value=True))
        p.emit("vector", lambda e: e.drain(), inc=[(s_sx, 1)])
        p.emit("gpsimd", lambda e: e.memset(lmax_sb[:, :], 0.0))
        p.emit("gpsimd", lambda e: e.dma_start(out=lmax_sb[0:1, 0:128], in_=redt[:, 0:1],
                                               allow_slow_non_contiguous=True),
               wait=[(s_sx, s_sx.now())], inc=[(s_cdma, 16)])
        p.emit("vector", lambda e: e.reduce_max(lmax_sb[0:1, 128:129], lmax_sb[0:1, 0:128], AX),
               wait=[(s_cdma, s_cdma.now())])
        p.emit("vector", lambda e: e.drain(), inc=[(s_sx, 1)])
        p.emit("gpsimd", lambda e: e.dma_start(out=d_arx_i.ap(), in_=lmax_sb[0:1, 128:144]),
               wait=[(s_sx, s_sx.now())], inc=[(s_cdma, 16)])
        p.emit("gpsimd", lambda e: e.collective_compute(
            "AllReduce", ALU.max, replica_groups=[list(range(8))],
            ins=[d_arx_i.ap()], outs=[d_arx_o.ap()]),
            wait=[(s_cdma, s_cdma.now())], inc=[(s_cc, 1)])
        p.emit("gpsimd", lambda e: e.dma_start(out=gmax_sb[:, :], in_=d_arx_o.ap()),
               wait=[(s_cc, s_cc.now())], inc=[(s_cdma, 16)])
        p.emit("vector", lambda e: e.tensor_tensor(scal[0:1, 0:1], gmax_sb[0:1, 0:1],
                                                   sig_sb[0:1, 0:1], ALU.mult),
               wait=[(s_cdma, s_cdma.now()), (s_sx, 1)])
        p.emit("vector", lambda e: e.drain())
        p.emit("vector", lambda e: e.reciprocal(scal[0:1, 1:2], scal[0:1, 0:1]))
        p.emit("vector", lambda e: e.drain())
        p.emit("vector", lambda e: e.tensor_scalar(scal[0:1, 1:2], scal[0:1, 1:2],
                                                   QMAX, 0.0, ALU.mult, ALU.add))
        p.emit("vector", lambda e: e.tensor_scalar(scal[0:1, 2:3], scal[0:1, 0:1],
                                                   1.0 / QMAX, 0.0, ALU.mult, ALU.add))
        p.emit("vector", lambda e: e.drain(), inc=[(s_sx, 1)])
        return xn_done, kv_done, kn_done

    def emit_sx_bcast():
        p.emit("tensor", lambda e: e.matmul(ps_b3[:, 0:1], ones1f[0:1, :],
                                            scal[0:1, 1:2], start=True, stop=True),
               wait=[(s_sx, s_sx.now()), (s_bc, s_bc.now()),
                     (s_qev, s_qev.now()), (s_nrm, s_nrm.now())],
               inc=[(s_bc, 1)])
        p.emit("tensor", lambda e: e.matmul(ps_b3[:, 1:2], ones1f[0:1, :],
                                            scal[0:1, 2:3], start=True, stop=True),
               inc=[(s_bc, 1)])
        p.emit("vector", lambda e: e.tensor_copy(rsx_pm[:, :], ps_b3[:, 0:1]),
               wait=[(s_bc, s_bc.now())], inc=[(s_bc, 1)])
        p.emit("vector", lambda e: e.tensor_copy(sx_pm[:, :], ps_b3[:, 1:2]))
        p.emit("vector", lambda e: e.drain(), inc=[(s_bc, 1)])
        marks["sxb"] = s_bc.now()

    def emit_scale_prepass():
        jobs = ([(d_wgn, ic, rmg, DD) for ic in range(ICN)] +
                [(d_wun, ic, rmu, DD) for ic in range(ICN)] +
                [(d_wdn, hc, rmd, II) for hc in range(DC)])
        segmap = []
        col = 0
        for (w, ch, dst, wide) in jobs:
            cols = []
            pos = 0
            while pos < wide:
                seg = min(512, wide - pos)
                p.emit("gpsimd", lambda e, w=w, ch=ch, pos=pos, seg=seg: e.dma_start(
                    out=am32[:, NATF:NATF + seg],
                    in_=w[ch * 128:(ch + 1) * 128, pos:pos + seg]),
                    wait=[(s_scl, s_scl.now())], inc=[(s_nat, 16)])
                p.emit("vector", lambda e, seg=seg, col=col: e.reduce_max(
                    segred[:, col:col + 1], am32[:, NATF:NATF + seg], AX,
                    apply_absolute_value=True), wait=[(s_nat, s_nat.now())],
                    inc=[(s_scl, 1)])
                cols.append(col)
                col += 1
                pos += seg
            segmap.append((dst, ch, cols))
        for (dst, ch, cols) in segmap:
            lo, hi = cols[0], cols[-1] + 1
            p.emit("vector", lambda e, dst=dst, ch=ch, lo=lo, hi=hi: e.reduce_max(
                dst[:, ch:ch + 1], segred[:, lo:hi], AX))
        p.emit("vector", lambda e: e.drain())
        for (srct, dst, n) in ((rmg, rgc, 22), (rmu, ruc, 22), (rmd, rdc, 8)):
            p.emit("vector", lambda e, srct=srct, dst=dst, n=n: e.reciprocal(
                dst[:, 0:n], srct[:, 0:n]))
            p.emit("vector", lambda e: e.drain())
            p.emit("vector", lambda e, dst=dst, n=n: e.tensor_scalar(
                dst[:, 0:n], dst[:, 0:n], QMAX, 0.0, ALU.mult, ALU.add))
            p.emit("vector", lambda e: e.drain())
        p.emit("vector", lambda e: e.engine_nop() if hasattr(e, "engine_nop") else e.nop(),
               inc=[(s_scl, 1)])

    def emit_qgen(qg, extra_waits=()):
        buf = qg % 2
        for ec in range(ECK):
            for dc in range(DC):
                def qmm(e, qg=qg, ec=ec, dc=dc):
                    t0 = qg * 512
                    if qg < 4:
                        rhs = xn_own[:, dc * 2048 + t0:dc * 2048 + t0 + 512]
                    else:
                        rhs = am16[:, XNP + dc * 2048 + t0 - 2048:
                                   XNP + dc * 2048 + t0 - 2048 + 512]
                    return e.matmul(ps_b3[:, :],
                                    wq_sb[:, dc * 512 + ec * 128:dc * 512 + (ec + 1) * 128],
                                    rhs, start=(dc == 0), stop=(dc == 7))
                w = list(extra_waits) if (ec == 0 and dc == 0) else []
                if dc == 0:
                    w.append((s_qev, s_qev.now()))
                    w.append((s_bc, s_bc.now()))
                    if ec == 0:
                        w.append((s_nrm, s_nrm.now()))
                p.emit("tensor", qmm, wait=w, inc=([(s_qmm, 1)] if dc == 7 else []))
            p.emit("vector", lambda e, buf=buf, ec=ec: e.tensor_copy(
                am16[:, QOD + buf * 2048 + ec * 512:QOD + buf * 2048 + (ec + 1) * 512],
                ps_b3[:, :]), wait=[(s_qmm, s_qmm.now())])
            p.emit("vector", lambda e: e.drain(), inc=[(s_qev, 1)])

    def emit_xq():
        for dc in range(DC):
            for tg in range(4):
                off = dc * 2048 + tg * 512
                p.emit("vector", lambda e, off=off: e.tensor_scalar(
                    scr32[:, :], xn_own[:, off:off + 512], rsx_pm[:, 0:1], MAGIC,
                    ALU.mult, ALU.add),
                    wait=[(s_bc, marks["sxb"]), (s_qev, s_qev.now())])
                p.emit("vector", lambda e, off=off: e.tensor_scalar(
                    xn_own[:, off:off + 512], scr32[:, :], -MAGIC, QMAX,
                    ALU.add, ALU.min))
                p.emit("vector", lambda e, off=off: e.tensor_scalar_max(
                    xn_own[:, off:off + 512], xn_own[:, off:off + 512], -QMAX - 1.0),
                    inc=[(s_xq, 1)])

    def phase_B(xn_done, kv_done, kn_done):
        emit_qgen(0, [(s_kvev, kv_done)])
        qgen_emitted = 1
        xq_emitted = False
        for qg in range(QG):
            qbuf = qg % 2
            q_ready = s_qev.now()
            for grp in range(2):
                p.emit("vector", lambda e, qbuf=qbuf, grp=grp: e.tensor_tensor(
                    junkb[:, 0:1024],
                    am16[:, QOD + qbuf * 2048 + grp * 1024:QOD + qbuf * 2048 + (grp + 1) * 1024],
                    am16[:, QOD + qbuf * 2048 + grp * 1024:QOD + qbuf * 2048 + (grp + 1) * 1024],
                    ALU.mult), wait=[(s_qev, q_ready)])
                p.emit("vector", lambda e: e.drain(), inc=[(s_aug, 1)])
                for h in range(4):
                    par = h % 2
                    prq = h // 2
                    def qsqmm(e, par=par, prq=prq, h=h):
                        return e.matmul(
                            ps_sum[32 * h:32 * h + 32, :],
                            onesb[64 * par:64 * par + 64, 0:32],
                            junkb[64 * par:64 * par + 64, prq * 512:(prq + 1) * 512],
                            start=True, stop=True,
                            tile_position=(64 * par, 32 * h))
                    w = []
                    if h == 0:
                        w = [(s_aug, s_aug.now()), (s_nrm, s_nrm.now())]
                    p.emit("tensor", qsqmm, wait=w,
                           inc=([(s_summ, 1)] if h == 3 else []))
                p.emit("vector", lambda e: e.tensor_copy(scr32[:, :], ps_sum[:, :]),
                       wait=[(s_summ, s_summ.now())])
                p.emit("vector", lambda e: e.drain(), inc=[(s_aug, 1)])
                p.emit("scalar", lambda e: e.activation(am[:, AOEV:AOEV + 512],
                                                        scr32[:, :], ACTF.Ln),
                       wait=[(s_aug, s_aug.now())])
                p.emit("scalar", lambda e: e.drain(), inc=[(s_aug, 1)])
                p.emit("scalar", lambda e: e.activation(am[:, QAUG:QAUG + 512],
                                                        am[:, AOEV:AOEV + 512],
                                                        ACTF.Exp, scale=halfc[:, 0:1],
                                                        bias=lnmk_pm[:, 0:1]),
                       wait=[(s_aug, s_aug.now()), (s_scm, s_scm.now())])
                p.emit("scalar", lambda e: e.drain(), inc=[(s_aug, 1)])
                p.emit("vector", lambda e: e.tensor_scalar(
                    am[:, QAUG:QAUG + 512], am[:, QAUG:QAUG + 512], -1.0, 0.0,
                    ALU.mult, ALU.add),
                    wait=[(s_aug, s_aug.now())])
                p.emit("vector", lambda e: e.drain(), inc=[(s_aug, 1)])
                aug_ready = s_aug.now()
                av_ckpt = []
                summ_ckpt = []
                for kc in range(KC):
                    pbuf = kc % 2
                    for pr in range(2):
                        ec = grp * 2 + pr
                        for par in range(2):
                            h = 2 * pr + par
                            def smm(e, ec=ec, par=par, kc=kc, h=h, qbuf=qbuf):
                                lo = par * 64
                                return e.matmul(
                                    ps_sc[:, h * 512:(h + 1) * 512],
                                    am16[lo:lo + 64, KF + ec * 4096 + kc * 128:
                                         KF + ec * 4096 + (kc + 1) * 128],
                                    am16[lo:lo + 64, QOD + qbuf * 2048 + ec * 512:
                                         QOD + qbuf * 2048 + (ec + 1) * 512],
                                    start=True, stop=False, tile_position=(lo, 0))
                            w = []
                            if pr == 0 and par == 0:
                                w = [(s_exp, s_exp.now()), (s_qev, q_ready)]
                                if kc == 0:
                                    w.append((s_aug, aug_ready))
                                if kc == 0 and qg == 0 and grp == 0:
                                    w += [(s_kvev, kv_done), (s_xev, xn_done)]
                            p.emit("tensor", smm, wait=w)
                    for h in range(4):
                        def augmm(e, h=h):
                            return e.matmul(
                                ps_sc[:, h * 512:(h + 1) * 512],
                                onesb[32 * h:32 * h + 1, 0:128],
                                am[32 * h:32 * h + 1, QAUG:QAUG + 512],
                                start=False, stop=True, tile_position=(32 * h, 0))
                        p.emit("tensor", augmm,
                               inc=([(s_scm, 1)] if h == 3 else []))
                    def expfn(e, pbuf=pbuf):
                        return e.activation(am[:, PT + pbuf * 2048:PT + pbuf * 2048 + 2048],
                                            ps_sc[:, :], ACTF.Exp,
                                            scale=pmc[:, 0:1], bias=pmc[:, 1:2])
                    ew = [(s_scm, s_scm.now())]
                    if kc >= 2:
                        ew.append((s_summ, summ_ckpt[kc - 2]))
                    elif qg + grp > 0:
                        ew.append((s_summ, s_summ.now()))
                    p.emit("scalar", expfn, wait=ew, inc=[(s_exp, 1)])
                    for pr in range(2):
                        for par in range(2):
                            h = 2 * pr + par
                            hd = grp * 4 + h
                            def avmm(e, pr=pr, par=par, h=h, hd=hd, kc=kc, pbuf=pbuf):
                                return e.matmul(
                                    ps_av[par * 64:(par + 1) * 64, pr * 512:pr * 512 + 512],
                                    vt[:, kc * 512 + hd * 64:kc * 512 + (hd + 1) * 64],
                                    am[:, PT + pbuf * 2048 + h * 512:
                                       PT + pbuf * 2048 + (h + 1) * 512],
                                    start=(kc == 0), stop=(kc == KC - 1),
                                    tile_position=(0, par * 64))
                            w = []
                            if pr == 0 and par == 0:
                                w = [(s_exp, s_exp.now())]
                                if kc == 0:
                                    w.append((s_nrm, s_nrm.now()))
                            p.emit("tensor", avmm, wait=w,
                                   inc=([(s_avm, 1)] if (pr == 1 and par == 1) else []))
                    for h in range(4):
                        def summ(e, h=h, kc=kc, pbuf=pbuf):
                            return e.matmul(ps_sum[h * 32:(h + 1) * 32, :],
                                            onesb[:, 0:32],
                                            am[:, PT + pbuf * 2048 + h * 512:
                                               PT + pbuf * 2048 + (h + 1) * 512],
                                            start=(kc == 0), stop=(kc == KC - 1),
                                            tile_position=(0, h * 32))
                        p.emit("tensor", summ, inc=([(s_summ, 1)] if h == 3 else []))
                    av_ckpt.append(s_avm.now())
                    summ_ckpt.append(s_summ.now())
                    if grp == 0 and kc == 16 and qgen_emitted <= qg + 1 and qgen_emitted < QG:
                        emit_qgen(qgen_emitted)
                        qgen_emitted += 1
                p.emit("vector", lambda e: e.reciprocal(am32[:, RSUM32:RSUM32 + 512],
                                                        ps_sum[:, :]),
                       wait=[(s_summ, s_summ.now())], inc=[(s_nrm, 1)])
                for pr in range(2):
                    sel = selA if pr == 0 else selB
                    p.emit("tensor", lambda e, sel=sel: e.matmul(
                        ps_b3[:, :], sel[:, :], am32[:, RSUM32:RSUM32 + 512],
                        start=True, stop=True),
                        wait=[(s_nrm, s_nrm.now()), (s_qev, s_qev.now()),
                              (s_bc, s_bc.now())],
                        inc=[(s_bc, 1)])
                    p.emit("vector", lambda e, pr=pr: e.tensor_copy(
                        am[:, AOEV:AOEV + 512],
                        ps_av[:, pr * 512:(pr + 1) * 512]),
                        wait=[(s_avm, s_avm.now())])
                    def ttn(e, pr=pr, grp=grp, qg=qg):
                        ec = grp * 2 + pr
                        return e.tensor_tensor(
                            ao16[:, ec * 4096 + qg * 512:ec * 4096 + (qg + 1) * 512],
                            am[:, AOEV:AOEV + 512],
                            ps_b3[:, :], ALU.mult)
                    p.emit("vector", ttn, wait=[(s_bc, s_bc.now())], inc=[(s_nrm, 1)])
                if BARRIER_LEVEL >= 3:
                    barrier()
            if qg == 3 and not xq_emitted:
                emit_sx_bcast()
                emit_xq()
                xq_emitted = True
            if BARRIER_LEVEL >= 2:
                barrier()
        return s_nrm.now()

    def phase_OP(att_done):
        qg7 = s_qev.now()
        p.emit("gpsimd", lambda e: e.dma_start(out=am16[:, WO:WO + 8192], in_=d_wo),
               wait=[(s_qev, qg7)], inc=[(s_ld, 16)])
        wo_ld = s_ld.now()
        p.emit("gpsimd", lambda e: e.dma_start(
            out=d_ag_i.ap(), in_=bass.AP(ao16, 2048, [[16384, 128], [4096, 4], [1, 2048]])),
            wait=[(s_nrm, att_done)], inc=[(s_cdma, 16)])
        p.emit("gpsimd", lambda e: e.collective_compute(
            "AllGather", ALU.bypass, replica_groups=[[0, 1], [2, 3], [4, 5], [6, 7]],
            ins=[d_ag_i.ap()], outs=[d_ag_o.ap()]),
            wait=[(s_cdma, s_cdma.now())], inc=[(s_cc, 1)])

        def agread(e):
            with e.register("agr") as gr:
                e.reg_load(gr, psel_sb[0:1, 0:1])
                e.reg_mul(gr, gr, 128 * 8192)
                return e.dma_start(out=am16[:, AOPART:AOPART + 8192],
                                   in_=bass.AP(d_ag_o, gr, [[8192, 128], [1, 8192]]))
        p.emit("gpsimd", agread, wait=[(s_cc, s_cc.now()), (s_qev, qg7)],
               inc=[(s_cdma, 16)])
        ag_done = s_cdma.now()
        for tg in range(4):
            for dchunk in range(DC):
                bank = (tg * DC + dchunk) % 2
                for ec in range(8):
                    def opmm(e, tg=tg, dchunk=dchunk, ec=ec, bank=bank):
                        if ec < 4:
                            rhs = ao16[:, ec * 4096 + tg * 512:ec * 4096 + tg * 512 + 512]
                        else:
                            rhs = am16[:, AOPART + (ec - 4) * 2048 + tg * 512:
                                       AOPART + (ec - 4) * 2048 + tg * 512 + 512]
                        return e.matmul(ps_av[:, bank * 512:bank * 512 + 512],
                                        am16[:, WO + ec * 1024 + dchunk * 128:
                                             WO + ec * 1024 + (dchunk + 1) * 128],
                                        rhs, start=(ec == 0), stop=(ec == 7))
                    w = []
                    if ec == 0:
                        w = [(s_rep, max(0, s_rep.now() - 1))]
                        if tg == 0 and dchunk == 0:
                            w += [(s_cdma, ag_done), (s_ld, wo_ld),
                                  (s_nrm, att_done), (s_avm, s_avm.now())]
                    p.emit("tensor", opmm, wait=w, inc=([(s_op, 1)] if ec == 7 else []))
                p.emit("vector", lambda e, tg=tg, dchunk=dchunk, bank=bank: e.tensor_scalar(
                    r_r[:, dchunk * 2048 + tg * 512:dchunk * 2048 + tg * 512 + 512],
                    ps_av[:, bank * 512:bank * 512 + 512],
                    bout_f[:, dchunk:dchunk + 1], 0.0, ALU.add, ALU.add),
                    wait=[(s_op, s_op.now())], inc=[(s_rep, 1)])
        return s_rep.now()

    def phase_C1(op_done):
        p.emit("vector", lambda e: e.tensor_scalar(sgx[:, :], rmg[:, :], sx_pm[:, 0:1],
                                                   1.0 / QMAX, ALU.mult, ALU.mult),
               wait=[(s_bc, marks["sxb"]), (s_scl, s_scl.now())])
        p.emit("vector", lambda e: e.tensor_scalar(sux[:, :], rmu[:, :], sx_pm[:, 0:1],
                                                   1.0 / QMAX, ALU.mult, ALU.mult))
        p.emit("vector", lambda e: e.drain(), inc=[(s_scl, 1)])
        sgx_done = s_scl.now()
        for ic in range(ICN):
            buf = ic % 2
            p.emit("sync", lambda e, ic=ic, buf=buf: e.dma_start(
                out=ao32[:, WGST + buf * 1024:WGST + (buf + 1) * 1024],
                in_=d_wgt[ic * 128:(ic + 1) * 128, :]),
                wait=([(s_rep, op_done), (s_tap, s_tap.now())] if ic == 0 else
                      [(s_wq, max(0, s_wq.now() - 2))]),
                inc=[(s_wl, 16)])
            p.emit("sync", lambda e, ic=ic, buf=buf: e.dma_start(
                out=ao32[:, WUST + buf * 1024:WUST + (buf + 1) * 1024],
                in_=d_wut[ic * 128:(ic + 1) * 128, :]),
                inc=[(s_wl, 16)])
            p.emit("gpsimd", lambda e, ic=ic: e.dma_start(
                out=rstg[0:1, 0:128], in_=rgc[:, ic:ic + 1],
                allow_slow_non_contiguous=True),
                wait=[(s_scl, sgx_done), (s_wq, s_wq.now())], inc=[(s_nat, 16)])
            p.emit("gpsimd", lambda e, ic=ic: e.dma_start(
                out=rstg[0:1, 128:256], in_=ruc[:, ic:ic + 1],
                allow_slow_non_contiguous=True), inc=[(s_nat, 16)])
            p.emit("tensor", lambda e: e.matmul(ps_b3[:, 0:128], ones1f[0:1, :],
                                                rstg[0:1, 0:128], start=True, stop=True),
                   wait=[(s_nat, s_nat.now()), (s_wq, s_wq.now()),
                         (s_bc, s_bc.now()), (s_rep, op_done)],
                   inc=[(s_bc, 1)])
            p.emit("tensor", lambda e: e.matmul(ps_b3[:, 128:256], ones1f[0:1, :],
                                                rstg[0:1, 128:256], start=True, stop=True),
                   inc=[(s_bc, 1)])
            p.emit("vector", lambda e: e.tensor_copy(bcq[:, :], ps_b3[:, 0:256]),
                   wait=[(s_wl, s_wl.now()), (s_bc, s_bc.now())])
            p.emit("vector", lambda e: e.drain())
            for WST, WI in ((WGST, WGI), (WUST, WUI)):
                coff = 0 if WST == WGST else 128
                for hh in range(2):
                    rep = bass.AP(bcq, coff, [[256, 128], [0, 4], [1, 128]])
                    p.emit("vector", lambda e, WST=WST, buf=buf, hh=hh, rep=rep:
                           e.tensor_tensor(
                               scr32[:, :],
                               ao32[:, WST + buf * 1024 + hh * 512:
                                    WST + buf * 1024 + hh * 512 + 512],
                               rep, ALU.mult))
                    p.emit("vector", lambda e: e.tensor_scalar(
                        scr32[:, :], scr32[:, :], MAGIC, -MAGIC, ALU.add, ALU.add))
                    p.emit("vector", lambda e, WI=WI, buf=buf, hh=hh: e.tensor_scalar(
                        ao16[:, WI + buf * 1024 + hh * 512:WI + buf * 1024 + hh * 512 + 512],
                        scr32[:, :], QMAX, -QMAX - 1.0, ALU.min, ALU.max),
                        inc=([(s_wq, 1)] if hh == 1 else []))
            wq_now = s_wq.now()
            for tg in range(4):
                gbank = (ic * 4 + tg) % 2
                for j in range(DC):
                    def gmm(e, ic=ic, tg=tg, j=j, buf=buf, gbank=gbank):
                        return e.matmul(ps_sc[:, gbank * 512:gbank * 512 + 512],
                                        ao16[:, WGI + buf * 1024 + j * 128:
                                             WGI + buf * 1024 + (j + 1) * 128],
                                        xn_own[:, j * 2048 + tg * 512:
                                               j * 2048 + tg * 512 + 512],
                                        start=(j == 0), stop=(j == 7))
                    w = []
                    if j == 0:
                        w = [(s_wq, wq_now), (s_fep, max(0, s_fep.now() - 4))]
                        if ic == 0 and tg == 0:
                            w += [(s_xq, s_xq.now()), (s_exp, s_exp.now())]
                    p.emit("tensor", gmm, wait=w, inc=([(s_fmm, 1)] if j == 7 else []))
                sgb = sg_sb if (ic * 4 + tg) % 2 == 0 else ep_sb
                p.emit("scalar", lambda e, ic=ic, gbank=gbank, sgb=sgb: e.activation(
                    sgb[:, 0:512], ps_sc[:, gbank * 512:gbank * 512 + 512],
                    ACTF.Silu, scale=sgx[:, ic:ic + 1]),
                    wait=[(s_fmm, s_fmm.now()), (s_scl, sgx_done),
                          (s_fep, max(0, s_fep.now() - 4))], inc=[(s_fep, 1)])
                ubank = 2 + (ic * 4 + tg) % 2
                for j in range(DC):
                    def umm(e, ic=ic, tg=tg, j=j, buf=buf, ubank=ubank):
                        return e.matmul(ps_sc[:, ubank * 512:ubank * 512 + 512],
                                        ao16[:, WUI + buf * 1024 + j * 128:
                                             WUI + buf * 1024 + (j + 1) * 128],
                                        xn_own[:, j * 2048 + tg * 512:
                                               j * 2048 + tg * 512 + 512],
                                        start=(j == 0), stop=(j == 7))
                    p.emit("tensor", umm,
                           wait=([(s_fep, max(0, s_fep.now() - 4))] if j == 0 else []),
                           inc=([(s_fmm, 1)] if j == 7 else []))
                def sttm(e, ic=ic, tg=tg, ubank=ubank, sgb=sgb):
                    return e.scalar_tensor_tensor(
                        am16[:, ic * 2048 + tg * 512:ic * 2048 + (tg + 1) * 512],
                        ps_sc[:, ubank * 512:ubank * 512 + 512], sux[:, ic:ic + 1],
                        sgb[:, 0:512], ALU.mult, ALU.mult)
                p.emit("vector", sttm,
                       wait=[(s_fmm, s_fmm.now()), (s_fep, s_fep.now())],
                       inc=[(s_fep, 1)])
                p.emit("vector", lambda e, ic=ic, tg=tg: e.reduce_max(
                    mwide[:, ic * 4 + tg:ic * 4 + tg + 1],
                    am16[:, ic * 2048 + tg * 512:ic * 2048 + (tg + 1) * 512],
                    AX, apply_absolute_value=True),
                    inc=[(s_fep, 1)])
        barrier()
        p.emit("vector", lambda e: e.drain(), wait=[(s_fep, s_fep.now())])
        p.emit("vector", lambda e: e.reduce_max(mwide[:, 88:89], mwide[:, 0:88], AX))
        p.emit("vector", lambda e: e.drain(), inc=[(s_fep, 1)])
        m_done = s_fep.now()
        p.emit("gpsimd", lambda e: e.memset(hmax_sb[:, :], 0.0))
        p.emit("gpsimd", lambda e: e.dma_start(out=hmax_sb[0:1, 0:128], in_=mwide[:, 88:89],
                                               allow_slow_non_contiguous=True),
               wait=[(s_fep, m_done)], inc=[(s_cdma, 16)])
        p.emit("vector", lambda e: e.reduce_max(hmax_sb[0:1, 128:129], hmax_sb[0:1, 0:128], AX),
               wait=[(s_cdma, s_cdma.now())])
        p.emit("vector", lambda e: e.drain(), inc=[(s_sh, 1)])
        p.emit("gpsimd", lambda e: e.dma_start(out=d_arh_i.ap(), in_=hmax_sb[0:1, 128:144]),
               wait=[(s_sh, s_sh.now())], inc=[(s_cdma, 16)])
        p.emit("gpsimd", lambda e: e.collective_compute(
            "AllReduce", ALU.max, replica_groups=[list(range(8))],
            ins=[d_arh_i.ap()], outs=[d_arh_o.ap()]),
            wait=[(s_cdma, s_cdma.now())], inc=[(s_cc, 1)])
        p.emit("gpsimd", lambda e: e.dma_start(out=ghmax_sb[:, :], in_=d_arh_o.ap()),
               wait=[(s_cc, s_cc.now())], inc=[(s_cdma, 16)])
        p.emit("vector", lambda e: e.tensor_tensor(scal[0:1, 4:5], ghmax_sb[0:1, 0:1],
                                                   sig_sb[0:1, 1:2], ALU.mult),
               wait=[(s_cdma, s_cdma.now())])
        p.emit("vector", lambda e: e.drain())
        p.emit("vector", lambda e: e.reciprocal(scal[0:1, 5:6], scal[0:1, 4:5]))
        p.emit("vector", lambda e: e.drain())
        p.emit("vector", lambda e: e.tensor_scalar(scal[0:1, 5:6], scal[0:1, 5:6],
                                                   QMAX, 0.0, ALU.mult, ALU.add))
        p.emit("vector", lambda e: e.tensor_scalar(scal[0:1, 6:7], scal[0:1, 4:5],
                                                   1.0 / QMAX, 0.0, ALU.mult, ALU.add))
        p.emit("vector", lambda e: e.drain(), inc=[(s_sh, 1)])
        p.emit("tensor", lambda e: e.matmul(ps_b3[:, 0:1], ones1f[0:1, :],
                                            scal[0:1, 5:6], start=True, stop=True),
               wait=[(s_sh, s_sh.now()), (s_bc, s_bc.now()), (s_fmm, s_fmm.now())],
               inc=[(s_bc, 1)])
        p.emit("tensor", lambda e: e.matmul(ps_b3[:, 1:2], ones1f[0:1, :],
                                            scal[0:1, 6:7], start=True, stop=True),
               inc=[(s_bc, 1)])
        p.emit("vector", lambda e: e.tensor_copy(rsh_pm[:, :], ps_b3[:, 0:1]),
               wait=[(s_bc, s_bc.now())], inc=[(s_bc, 1)])
        p.emit("vector", lambda e: e.tensor_copy(sh_pm[:, :], ps_b3[:, 1:2]))
        p.emit("vector", lambda e: e.drain(), inc=[(s_bc, 1)])
        marks["shb"] = s_bc.now()
        p.emit("vector", lambda e: e.tensor_scalar(sdx[:, :], rmd[:, :], sh_pm[:, 0:1],
                                                   1.0 / QMAX, ALU.mult, ALU.mult),
               wait=[(s_bc, marks["shb"])])
        p.emit("vector", lambda e: e.drain(), inc=[(s_sh, 1)])
        return m_done

    def phase_C2(m_done):
        sh_done = s_sh.now()
        for ic in range(ICN):
            for tg in range(4):
                off = ic * 2048 + tg * 512
                p.emit("vector", lambda e, off=off: e.tensor_scalar(
                    scr32[:, :], am16[:, off:off + 512], rsh_pm[:, 0:1], MAGIC,
                    ALU.mult, ALU.add),
                    wait=([(s_bc, marks["shb"]), (s_fep, m_done), (s_tap, s_tap.now())]
                          if (ic == 0 and tg == 0) else []))
                p.emit("vector", lambda e, off=off: e.tensor_scalar(
                    am16[:, off:off + 512], scr32[:, :], -MAGIC, QMAX, ALU.add, ALU.min))
                p.emit("vector", lambda e, off=off: e.tensor_scalar_max(
                    am16[:, off:off + 512], am16[:, off:off + 512], -QMAX - 1.0),
                    inc=[(s_mq, 1)])
        mq_done = s_mq.now()
        for hc in range(DC):
            buf = hc % 2
            p.emit("sync", lambda e, hc=hc, buf=buf: e.dma_start(
                out=ao32[:, buf * 2816:(buf + 1) * 2816],
                in_=d_wdt[hc * 128:(hc + 1) * 128, :]),
                wait=([(s_fep, m_done), (s_wl, s_wl.now())] if hc == 0 else
                      [(s_wq, max(0, s_wq.now() - 1))]),
                inc=[(s_wl, 16)])
            p.emit("gpsimd", lambda e, hc=hc: e.dma_start(
                out=rstgd[0:1, 0:128], in_=rdc[:, hc:hc + 1],
                allow_slow_non_contiguous=True),
                wait=[(s_wq, s_wq.now())], inc=[(s_nat, 16)])
            p.emit("tensor", lambda e: e.matmul(ps_b3[:, 0:128], ones1f[0:1, :],
                                                rstgd[0:1, 0:128], start=True, stop=True),
                   wait=[(s_nat, s_nat.now()), (s_bc, s_bc.now()),
                         (s_wq, s_wq.now()), (s_mq, mq_done)],
                   inc=[(s_bc, 1)])
            p.emit("vector", lambda e: e.tensor_copy(bcq[:, 0:128], ps_b3[:, 0:128]),
                   wait=[(s_wl, s_wl.now()), (s_bc, s_bc.now())])
            p.emit("vector", lambda e: e.drain())
            pos = 0
            while pos < II:
                seg = min(512, II - pos)
                rep = bass.AP(bcq, 0, [[256, 128], [0, seg // 128], [1, 128]])
                p.emit("vector", lambda e, buf=buf, pos=pos, seg=seg, rep=rep:
                       e.tensor_tensor(
                           scr32[:, 0:seg],
                           ao32[:, buf * 2816 + pos:buf * 2816 + pos + seg],
                           rep, ALU.mult))
                p.emit("vector", lambda e, seg=seg: e.tensor_scalar(
                    scr32[:, 0:seg], scr32[:, 0:seg], MAGIC, -MAGIC, ALU.add, ALU.add))
                p.emit("vector", lambda e, buf=buf, pos=pos, seg=seg: e.tensor_scalar(
                    arena_xn[:, WDI + buf * 2816 + pos:WDI + buf * 2816 + pos + seg],
                    scr32[:, 0:seg], QMAX, -QMAX - 1.0, ALU.min, ALU.max),
                    inc=([(s_wq, 1)] if pos + seg >= II else []))
                pos += seg
            wdq_now = s_wq.now()
            for tg in range(4):
                bank = (hc * 4 + tg) % 2
                for j in range(ICN):
                    def dmm(e, hc=hc, tg=tg, j=j, buf=buf, bank=bank):
                        return e.matmul(ps_av[:, bank * 512:bank * 512 + 512],
                                        arena_xn[:, WDI + buf * 2816 + j * 128:
                                                 WDI + buf * 2816 + (j + 1) * 128],
                                        am16[:, j * 2048 + tg * 512:j * 2048 + (tg + 1) * 512],
                                        start=(j == 0), stop=(j == ICN - 1))
                    w = []
                    if j == 0:
                        w = [(s_wq, wdq_now), (s_rep, max(0, s_rep.now() - 1))]
                        if hc == 0 and tg == 0:
                            w += [(s_mq, mq_done), (s_op, s_op.now())]
                    p.emit("tensor", dmm, wait=w,
                           inc=([(s_fmm, 1)] if j == ICN - 1 else []))
                jbuf = tg % 2
                p.emit("sync", lambda e, hc=hc, tg=tg, jbuf=jbuf: e.dma_start(
                    out=axn32[:, XJIT + jbuf * 512:XJIT + (jbuf + 1) * 512],
                    in_=d_xown[hc * 128:(hc + 1) * 128, tg * 512:(tg + 1) * 512]),
                    wait=([(s_rep, max(0, s_rep.now() - 1))] if (hc + tg) else
                          [(s_fep, m_done)]),
                    inc=[(s_jit, 16)])
                obuf = tg % 2
                p.emit("scalar", lambda e, hc=hc, bank=bank: e.activation(
                    ep_sb[:, 0:512], ps_av[:, bank * 512:bank * 512 + 512],
                    ACTF.Copy, scale=sdx[:, hc:hc + 1]),
                    wait=[(s_fmm, s_fmm.now()), (s_sh, s_sh.now()),
                          (s_rep, s_rep.now())], inc=[(s_fep, 1)])
                p.emit("vector", lambda e, hc=hc, tg=tg: e.tensor_tensor(
                    ep_sb[:, 512:1024], ep_sb[:, 0:512],
                    r_r[:, hc * 2048 + tg * 512:hc * 2048 + (tg + 1) * 512], ALU.add),
                    wait=[(s_fep, s_fep.now())])
                p.emit("vector", lambda e, jbuf=jbuf, obuf=obuf: e.tensor_tensor(
                    axn32[:, OUTSB + obuf * 512:OUTSB + (obuf + 1) * 512],
                    ep_sb[:, 512:1024],
                    axn32[:, XJIT + jbuf * 512:XJIT + (jbuf + 1) * 512], ALU.add),
                    wait=[(s_jit, s_jit.now()), (s_st, max(0, s_st.now() - 16))],
                    inc=[(s_rep, 1)])
                p.emit("sync", lambda e, hc=hc, tg=tg, obuf=obuf: e.dma_start(
                    out=d_out[hc * 128:(hc + 1) * 128, tg * 512:(tg + 1) * 512],
                    in_=axn32[:, OUTSB + obuf * 512:OUTSB + (obuf + 1) * 512]),
                    wait=[(s_rep, s_rep.now())], inc=[(s_st, 16)])

    s_jit = sem("jit")
    init_phase()
    barrier()
    xn_done, kv_done, kn_done = phase_A()
    if "xn" in taps:
        p.emit("gpsimd", lambda e: e.dma_start(out=tap_d["xn"], in_=xn_own[:, :]),
               wait=[(s_xev, xn_done)], inc=[(s_tap, 16)])
    if "xnp" in taps:
        p.emit("gpsimd", lambda e: e.dma_start(out=tap_d["xnp"], in_=am[:, XNP:XNP + 16384]),
               wait=[(s_xev, xn_done)], inc=[(s_tap, 16)])
    if "kf" in taps:
        p.emit("gpsimd", lambda e: e.dma_start(out=tap_d["kf"], in_=am[:, KF:KF + 16384]),
               wait=[(s_kvev, kv_done)], inc=[(s_tap, 16)])
    if "vt" in taps:
        p.emit("gpsimd", lambda e: e.dma_start(out=tap_d["vt"], in_=vt[:, :]),
               wait=[(s_kvev, kv_done)], inc=[(s_tap, 16)])
    if PREPASS_EARLY:
        emit_scale_prepass()
    if BARRIER_LEVEL >= 1:
        barrier()
    att_done = phase_B(xn_done, kv_done, kn_done)
    if not PREPASS_EARLY:
        emit_scale_prepass()
    if BARRIER_LEVEL >= 1:
        barrier()
    if "q0" in taps:
        p.emit("gpsimd", lambda e: e.dma_start(out=tap_d["q0"], in_=am[:, QOD:QOD + 2048]),
               wait=[(s_qev, 4)], inc=[(s_tap, 16)])
    if "ao" in taps:
        p.emit("gpsimd", lambda e: e.dma_start(out=tap_d["ao"], in_=ao[:, :]),
               wait=[(s_nrm, att_done)], inc=[(s_tap, 16)])
    if "xq" in taps:
        p.emit("gpsimd", lambda e: e.dma_start(out=tap_d["xq"], in_=xn_own[:, :]),
               wait=[(s_xq, s_xq.now())], inc=[(s_tap, 16)])
    op_done = phase_OP(att_done)
    if BARRIER_LEVEL >= 1:
        barrier()
    if "aop" in taps:
        p.emit("gpsimd", lambda e: e.dma_start(out=tap_d["aop"],
                                               in_=am[:, AOPART:AOPART + 8192]),
               wait=[(s_cdma, s_cdma.now())], inc=[(s_tap, 16)])
    if "r" in taps:
        p.emit("gpsimd", lambda e: e.dma_start(out=tap_d["r"], in_=r_r[:, :]),
               wait=[(s_rep, op_done)], inc=[(s_tap, 16)])
    m_done = phase_C1(op_done)
    if BARRIER_LEVEL >= 1:
        barrier()
    if "sc" in taps:
        def dump_sc(e):
            return e.dma_start(out=tap_d["sc"][0:6, 0:16].tensor[0:6, 0:16] if False else tap_d["sc"][0:1, 0:16], in_=lmax_sb[0:1, 0:16])
        p.emit("gpsimd", lambda e: e.dma_start(out=tap_d["sc"][0:1, 0:16], in_=lmax_sb[0:1, 0:16]),
               wait=[(s_sh, s_sh.now())], inc=[(s_tap, 16)])
        p.emit("gpsimd", lambda e: e.dma_start(out=tap_d["sc"][1:2, 0:16], in_=gmax_sb[0:1, 0:16]),
               inc=[(s_tap, 16)])
        p.emit("gpsimd", lambda e: e.dma_start(out=tap_d["sc"][2:3, 0:16], in_=hmax_sb[0:1, 0:16]),
               inc=[(s_tap, 16)])
        p.emit("gpsimd", lambda e: e.dma_start(out=tap_d["sc"][3:4, 0:16], in_=ghmax_sb[0:1, 0:16]),
               inc=[(s_tap, 16)])
        p.emit("gpsimd", lambda e: e.dma_start(out=tap_d["sc"][4:5, 0:16], in_=scal[0:1, 0:16]),
               inc=[(s_tap, 16)])
        p.emit("gpsimd", lambda e: e.dma_start(out=tap_d["sc"][5:6, 0:32], in_=xmax[0:1, 0:32]),
               inc=[(s_tap, 16)])
        p.emit("gpsimd", lambda e: e.dma_start(out=tap_d["sc"][6:7, 0:2], in_=rsx_pm[0:1, 0:1].tensor[0:1, 0:1] if False else redt[0:1, 0:2]),
               inc=[(s_tap, 16)])
    if "m" in taps:
        p.emit("gpsimd", lambda e: e.dma_start(out=tap_d["m"], in_=am[:, 0:45056]),
               wait=[(s_fep, m_done)], inc=[(s_tap, 16)])
    phase_C2(m_done)

    p.emit("sync", lambda e: e.nop(), wait=[(s_st, s_st.now())])
    if taps:
        p.emit("gpsimd", lambda e: e.nop(), wait=[(s_tap, s_tap.now())])

    p.replay()
    nc._sem_stack.close()
    return nc



def stage_inputs(inputs):
    x = np.asarray(inputs["L0_attn_x"], dtype=np.float32)
    w_qkv = np.asarray(inputs["L0_attn_w_qkv"], dtype=np.float32)
    w_out = np.asarray(inputs["L0_attn_w_out"], dtype=np.float32)
    b_out = np.asarray(inputs["L0_attn_b_out"], dtype=np.float32)
    clip_x = np.asarray(inputs["L0_ffn_clip_x"], dtype=np.float32)
    clip_down = np.asarray(inputs["L0_ffn_clip_down"], dtype=np.float32)
    w_gate = np.asarray(inputs["L0_ffn_w_gate"], dtype=np.float32)
    w_up = np.asarray(inputs["L0_ffn_w_up"], dtype=np.float32)
    w_down = np.asarray(inputs["L0_ffn_w_down"], dtype=np.float32)

    def blocked_qkv(sl):
        t = np.ascontiguousarray(sl.T)
        return np.ascontiguousarray(
            t.reshape(8, 128, 512).transpose(1, 0, 2).reshape(128, 4096))

    wg_tb = np.ascontiguousarray(
        w_gate.reshape(22, 128, 8, 128).transpose(0, 3, 2, 1).reshape(2816, 1024))
    wu_tb = np.ascontiguousarray(
        w_up.reshape(22, 128, 8, 128).transpose(0, 3, 2, 1).reshape(2816, 1024))
    wd_tb = np.ascontiguousarray(
        w_down.reshape(8, 128, 22, 128).transpose(0, 3, 2, 1).reshape(1024, 2816))
    clip2 = np.array([[clip_x[0], clip_down[0]]], dtype=np.float32)
    bpm = np.ascontiguousarray(b_out.reshape(8, 128).T).astype(np.float32)

    in_maps = []
    for c in range(8):
        b, h = c // 2, c % 2
        own = slice(h * 2048, (h + 1) * 2048)
        oth = slice((1 - h) * 2048, (2 - h) * 2048)
        xb = x[b]
        x_tm = np.ascontiguousarray(np.concatenate([xb[own], xb[oth]], axis=0))
        x_own_fm = np.ascontiguousarray(xb[own].T)
        wq = blocked_qkv(w_qkv[h * 512:(h + 1) * 512])
        wk = blocked_qkv(w_qkv[1024 + h * 512:1024 + (h + 1) * 512])
        wv = blocked_qkv(w_qkv[2048 + h * 512:2048 + (h + 1) * 512])
        colperm = np.concatenate([np.arange(h * 512, (h + 1) * 512),
                                  np.arange((1 - h) * 512, (2 - h) * 512)])
        woT = np.ascontiguousarray(w_out[:, colperm].T)
        wo_b = np.ascontiguousarray(
            woT.reshape(8, 128, 1024).transpose(1, 0, 2).reshape(128, 8192))
        in_maps.append({
            "x_tm": x_tm, "x_own_fm": x_own_fm,
            "wq_b": wq, "wk_b": wk, "wv_b": wv, "wo_b": wo_b,
            "wg_nat": w_gate, "wu_nat": w_up, "wd_nat": w_down,
            "wg_tb": wg_tb, "wu_tb": wu_tb, "wd_tb": wd_tb,
            "b_out_pm": bpm, "clip2": clip2,
            "psel": np.array([[1 - h]], dtype=np.uint32),
        })
    return in_maps


def kernel(**inputs):
    in_maps = stage_inputs(inputs)
    nc = build_program()
    res = run_bass_kernel_spmd(nc, in_maps, core_ids=list(range(8)))
    out = np.empty((4, 4096, 1024), dtype=np.float32)
    for c in range(8):
        b, h = c // 2, c % 2
        out[b, h * 2048:(h + 1) * 2048, :] = res.results[c]["out_fm"].T
    return out


# version 04 (final)
# speedup vs baseline: 1.0264x; 1.0264x over previous
import contextlib
import numpy as np
import concourse.bass as bass
import concourse.mybir as mybir
from concourse.bass_utils import run_bass_kernel_spmd

F32 = mybir.dt.float32
BF16 = mybir.dt.bfloat16
F16 = mybir.dt.float16
U32 = mybir.dt.uint32
I32 = mybir.dt.int32
ALU = mybir.AluOpType
ACTF = mybir.ActivationFunctionType
AX = mybir.AxisListType.X

MAGIC = float(np.float32(1.5 * 2 ** 23))
EPS = 1e-5
QMAX = 127.0
EXP_BIAS = -8.0

DD = 1024
DC = 8
ECK = 4
TCH = 32
KC = 32
QG = 8
II = 2816
ICN = 22
TOWN = 2048

ENGINES = ("tensor", "scalar", "vector", "gpsimd", "sync")


class Sem:
    def __init__(self, handle):
        self.h = handle
        self.n = 0

    def now(self):
        return self.n


class Prog:
    def __init__(self, nc):
        self.nc = nc
        self.ev = {e: [] for e in ENGINES}

    def emit(self, eng, fn, wait=(), inc=()):
        self.ev[eng].append((fn, tuple(wait), tuple(inc)))
        for s, n in inc:
            s.n += n

    def replay(self):
        nc = self.nc
        with nc.Block() as block:
            for name in ENGINES:
                evs = self.ev[name]

                def make(evs):
                    def run(eng):
                        last = {}
                        for fn, waits, incs in evs:
                            for s, v in waits:
                                if v > 0 and last.get(id(s), -1) < v:
                                    eng.wait_ge(s.h, v)
                                    last[id(s)] = v
                            inst = fn(eng)
                            for s, n in incs:
                                inst = inst.then_inc(s.h, n)
                    return run

                getattr(block, name)(make(evs))


PREPASS_EARLY = True
BARRIER_LEVEL = 3



def build_program(taps=()):
    nc = bass.Bass("TRN2", target_bir_lowering=False)
    taps = set(taps)

    d_xtm = nc.dram_tensor("x_tm", [4096, DD], F32, kind="ExternalInput").ap()
    d_xown = nc.dram_tensor("x_own_fm", [DD, TOWN], F32, kind="ExternalInput").ap()
    d_wq = nc.dram_tensor("wq_b", [128, 4096], F32, kind="ExternalInput").ap()
    d_wk = nc.dram_tensor("wk_b", [128, 4096], F32, kind="ExternalInput").ap()
    d_wv = nc.dram_tensor("wv_b", [128, 4096], F32, kind="ExternalInput").ap()
    d_wo = nc.dram_tensor("wo_b", [128, 8192], F32, kind="ExternalInput").ap()
    d_wgn = nc.dram_tensor("wg_nat", [II, DD], F32, kind="ExternalInput").ap()
    d_wun = nc.dram_tensor("wu_nat", [II, DD], F32, kind="ExternalInput").ap()
    d_wdn = nc.dram_tensor("wd_nat", [DD, II], F32, kind="ExternalInput").ap()
    d_wgt = nc.dram_tensor("wg_tb", [II, DD], F32, kind="ExternalInput").ap()
    d_wut = nc.dram_tensor("wu_tb", [II, DD], F32, kind="ExternalInput").ap()
    d_wdt = nc.dram_tensor("wd_tb", [DD, II], F32, kind="ExternalInput").ap()
    d_bout = nc.dram_tensor("b_out_pm", [128, 8], F32, kind="ExternalInput").ap()
    d_clip = nc.dram_tensor("clip2", [1, 2], F32, kind="ExternalInput").ap()
    d_psel = nc.dram_tensor("psel", [1, 1], U32, kind="ExternalInput").ap()
    d_out = nc.dram_tensor("out_fm", [DD, TOWN], F32, kind="ExternalOutput").ap()

    d_arx_i = nc.dram_tensor("arx_i", [1, 16], F32)
    d_arx_o = nc.dram_tensor("arx_o", [1, 16], F32)
    d_arh_i = nc.dram_tensor("arh_i", [1, 16], F32)
    d_arh_o = nc.dram_tensor("arh_o", [1, 16], F32)
    d_ag_i = nc.dram_tensor("ag_i", [128, 8192], F16)
    d_ag_o = nc.dram_tensor("ag_o", [256, 8192], F16)

    tap_d = {}
    for t in taps:
        shp = {"xn": [128, 16384], "xnp": [128, 16384], "kf": [128, 16384],
               "vt": [128, 16384], "ao": [128, 16384], "r": [128, 16384],
               "m": [128, 45056], "xq": [128, 16384], "q0": [128, 2048],
               "aop": [128, 8192], "sc": [128, 64]}[t]
        tap_d[t] = nc.dram_tensor("tap_" + t, shp, F32, kind="ExternalOutput").ap()

    A = nc.alloc_sbuf_tensor
    arena_xn = A("arena_xn", [128, 16384], F16)
    arena_m = A("arena_m", [128, 45056], BF16)
    arena_v = A("arena_v", [128, 16384], BF16)
    arena_ao = A("arena_ao", [128, 16384], BF16)
    wq_sb = A("wq_sb", [128, 4096], F16)

    scr32 = A("scr32", [128, 512], F32)
    junkb = scr32.bitcast(BF16)
    xntm = A("xntm", [128, 1024], F16)
    smcols = A("smcols", [128, 32], F32)
    sqcols = A("sqcols", [128, 32], F32)
    wrk = A("wrk", [128, 64], F32)
    rstd = A("rstd", [128, 32], F32)
    nmrc = A("nmrc", [128, 32], F32)
    segred = A("segred", [128, 144], F32)
    xmax = A("xmax", [128, 32], F32)
    mwide = segred
    selA = A("selA", [128, 128], F32)
    selB = A("selB", [128, 128], F32)
    ident = A("ident", [128, 128], F16)
    identi = A("identi", [128, 128], I32)
    onesb = A("onesb", [128, 128], BF16)
    ones1f = A("ones1f", [1, 128], F32)
    pmc = A("pmc", [128, 4], F32)
    bout_f = A("bout_f", [128, 8], F32)
    clip_sb = A("clip_sb", [1, 2], F32)
    sig_sb = A("sig_sb", [1, 2], F32)
    scal = A("scal", [1, 16], F32)
    lmax_sb = A("lmax_sb", [1, 144], F32)
    gmax_sb = A("gmax_sb", [1, 16], F32)
    hmax_sb = A("hmax_sb", [1, 144], F32)
    ghmax_sb = A("ghmax_sb", [1, 16], F32)
    mk_pm = A("mk_pm", [128, 1], F32)
    rsx_pm = A("rsx_pm", [128, 1], F32)
    sx_pm = A("sx_pm", [128, 1], F32)
    rsh_pm = A("rsh_pm", [128, 1], F32)
    sh_pm = A("sh_pm", [128, 1], F32)
    rmg = A("rmg", [128, 22], F32)
    rmu = A("rmu", [128, 22], F32)
    rmd = A("rmd", [128, 8], F32)
    rgc = A("rgc", [128, 22], F32)
    ruc = A("ruc", [128, 22], F32)
    rdc = A("rdc", [128, 8], F32)
    sgx = A("sgx", [128, 22], F32)
    sux = A("sux", [128, 22], F32)
    sdx = A("sdx", [128, 8], F32)
    rstg = A("rstg", [1, 256], F32)
    rstgd = A("rstgd", [1, 128], F32)
    psel_sb = A("psel_sb", [1, 1], U32)
    redt = A("redt", [128, 4], F32)
    bcq = A("bcq", [128, 256], F32)
    sg_sb = A("sg_sb", [128, 512], F16)
    ep_sb = A("ep_sb", [128, 1024], F16)

    am = arena_m
    am32 = arena_m.bitcast(F32)
    am16 = arena_m.bitcast(F16)
    ao = arena_ao
    ao32 = arena_ao.bitcast(F32)
    ao16 = arena_ao.bitcast(F16)
    vt = arena_v
    r_r = arena_v.bitcast(F16)
    xn_own = arena_xn
    axn32 = arena_xn.bitcast(F32)

    XNP = 0
    KF = 16384
    PT = 32768
    QOD = 36864
    SPARE = 40960
    RSUM32 = SPARE // 2
    AOEV = SPARE + 1024
    QAUG = SPARE + 2048
    NATF = (SPARE + 3072) // 2
    WKA = PT
    WVA = QOD
    WO = 0
    AOPART = 8192

    WGST = 0
    WUST = 2048
    WGI = 8192
    WUI = 12288
    XJIT = 0
    OUTSB = 1024
    WDI = 4096

    ps_av = nc.alloc_psum_tensor("ps_av", [128, 1024], F32)
    ps_sum = nc.alloc_psum_tensor("ps_sum", [128, 512], F32)
    ps_b3 = nc.alloc_psum_tensor("ps_b3", [128, 512], F32)
    ps_sc = nc.alloc_psum_tensor("ps_sc", [128, 2048], F32)
    ps_scb = ps_sc.bitcast(BF16)
    ps_sc16 = ps_sc.bitcast(F16)

    p = Prog(nc)
    marks = {}
    nc._sem_stack = contextlib.ExitStack()
    sems = {}

    def sem(name):
        if name not in sems:
            sems[name] = Sem(nc._sem_stack.enter_context(nc.semaphore(name)))
        return sems[name]

    s_ld = sem("ld")
    s_x = sem("xload")
    s_init = sem("init")
    s_stat = sem("stat")
    s_ntm = sem("ntm")
    s_act = sem("actn")
    s_tr = sem("tr")
    s_xev = sem("xev")
    s_kv = sem("kvmm")
    s_kvev = sem("kvev")
    s_qmm = sem("qmm")
    s_qev = sem("qev")
    s_scm = sem("scm")
    s_exp = sem("exp")
    s_avm = sem("avm")
    s_summ = sem("summ")
    s_nrm = sem("nrm")
    s_bc = sem("bc")
    s_cc = sem("cc")
    s_cdma = sem("cdma")
    s_sx = sem("sx")
    s_sh = sem("sh")
    s_wl = sem("wl")
    s_nat = sem("nat")
    s_scl = sem("scl")
    s_wq = sem("wqz")
    s_fmm = sem("fmm")
    s_fep = sem("fep")
    s_mq = sem("mq")
    s_xq = sem("xq")
    s_op = sem("opm")
    s_rep = sem("rep")
    s_st = sem("st")
    s_tap = sem("tapd")
    s_aug = sem("aug")
    s_kn = sem("kn")
    s_bar = sem("bar")
    barn = [0]

    def barrier():
        barn[0] += 1
        tgt = 5 * barn[0]
        for eng in ENGINES:
            if eng in ("tensor", "vector", "scalar"):
                p.emit(eng, lambda e: e.drain(), inc=[(s_bar, 1)])
            else:
                p.emit(eng, lambda e: e.nop(), inc=[(s_bar, 1)])
        for eng in ENGINES:
            p.emit(eng, lambda e, tgt=tgt: e.nop(), wait=[(s_bar, tgt)])

    def init_phase():
        g = "gpsimd"
        p.emit(g, lambda e: e.dma_start(out=clip_sb[:, :], in_=d_clip), inc=[(s_ld, 16)])
        p.emit(g, lambda e: e.dma_start(out=bout_f[:, :], in_=d_bout), inc=[(s_ld, 16)])
        p.emit(g, lambda e: e.dma_start(out=psel_sb[:, :], in_=d_psel), inc=[(s_ld, 16)])
        p.emit(g, lambda e: e.memset(onesb[:, :], 1.0))
        p.emit(g, lambda e: e.memset(ones1f[:, :], 1.0))
        p.emit(g, lambda e: e.memset(pmc[:, 0:1], 0.125))
        p.emit(g, lambda e: e.memset(pmc[:, 1:2], 0.0))
        p.emit(g, lambda e: e.memset(pmc[:, 2:3], EPS))
        p.emit(g, lambda e: e.memset(selA[:, :], 0.0))
        p.emit(g, lambda e: e.memset(selB[:, :], 0.0))
        p.emit(g, lambda e: e.memset(selA[0:1, 0:64], 1.0))
        p.emit(g, lambda e: e.memset(selA[32:33, 64:128], 1.0))
        p.emit(g, lambda e: e.memset(selB[64:65, 0:64], 1.0))
        p.emit(g, lambda e: e.memset(selB[96:97, 64:128], 1.0))
        p.emit(g, lambda e: e.iota(identi[:, :], [[1, 128]], channel_multiplier=-1,
                                   allow_small_or_imprecise_dtypes=True),
               inc=[(s_init, 1)])
        p.emit("vector", lambda e: e.tensor_single_scalar(ident[:, :], identi[:, :],
                                                          0.0, ALU.is_equal),
               wait=[(s_init, 1)], inc=[(s_init, 1)])
        p.emit(g, lambda e: e.dma_start(out=wq_sb[:, :], in_=d_wq), inc=[(s_ld, 16)])
        p.emit(g, lambda e: e.dma_start(out=am16[:, WKA:WKA + 4096], in_=d_wk),
               inc=[(s_ld, 16)])
        p.emit(g, lambda e: e.dma_start(out=am16[:, WVA:WVA + 4096], in_=d_wv),
               inc=[(s_ld, 16)])
        p.emit("scalar", lambda e: e.activation(sig_sb[:, :], clip_sb[:, :], ACTF.Sigmoid),
               wait=[(s_ld, 16 * 3)], inc=[(s_sx, 1)])
        p.emit("scalar", lambda e: e.activation(scal[0:1, 14:15], clip_sb[0:1, 0:1],
                                                ACTF.Sqrt))
        p.emit("scalar", lambda e: e.activation(scal[0:1, 14:15], clip_sb[0:1, 0:1],
                                                ACTF.Exp))
        p.emit("scalar", lambda e: e.activation(scal[0:1, 14:15], clip_sb[0:1, 0:1],
                                                ACTF.Silu))
        p.emit("scalar", lambda e: e.drain(), inc=[(s_init, 1)])

    def phase_A():
        for g in range(4):
            for i in range(8):
                tc = 8 * g + i
                p.emit("sync", lambda e, tc=tc, i=i: e.dma_start(
                    out=ao32[:, i * 1024:(i + 1) * 1024],
                    in_=d_xtm[tc * 128:(tc + 1) * 128, :]),
                    wait=([(s_ntm, 8 * (g - 1) + i + 1)] if g >= 1 else []),
                    inc=[(s_x, 16)])
                p.emit("vector", lambda e, i=i, tc=tc: e.tensor_scalar(
                    junkb[:, :], ao32[:, i * 1024:(i + 1) * 1024], 0.0, 0.0,
                    ALU.add, ALU.add, accum_out=smcols[:, tc:tc + 1]),
                    wait=[(s_x, s_x.now())])
                p.emit("vector", lambda e, i=i, tc=tc: e.scalar_tensor_tensor(
                    junkb[:, :], ao32[:, i * 1024:(i + 1) * 1024], 1.0,
                    ao32[:, i * 1024:(i + 1) * 1024], ALU.mult, ALU.mult,
                    accum_out=sqcols[:, tc:tc + 1]))
            c0 = 8 * g
            V = "vector"
            p.emit(V, lambda e: e.drain())
            p.emit(V, lambda e, c0=c0: e.tensor_scalar(
                wrk[:, 0:8], smcols[:, c0:c0 + 8], 1.0 / DD, 0.0, ALU.mult, ALU.add))
            p.emit(V, lambda e, c0=c0: e.tensor_scalar(
                wrk[:, 8:16], sqcols[:, c0:c0 + 8], 1.0 / DD, 0.0, ALU.mult, ALU.add))
            p.emit(V, lambda e: e.drain())
            p.emit(V, lambda e: e.tensor_tensor(wrk[:, 16:24], wrk[:, 0:8],
                                                wrk[:, 0:8], ALU.mult))
            p.emit(V, lambda e: e.drain())
            p.emit(V, lambda e: e.tensor_tensor(wrk[:, 24:32], wrk[:, 8:16],
                                                wrk[:, 16:24], ALU.subtract),
                   inc=[(s_stat, 1)])
            p.emit("scalar", lambda e: e.activation(wrk[:, 32:40], wrk[:, 24:32],
                                                    ACTF.Sqrt, bias=pmc[:, 2:3]),
                   wait=[(s_stat, s_stat.now()), (s_init, 1)])
            p.emit("scalar", lambda e: e.drain(), inc=[(s_act, 1)])
            p.emit(V, lambda e: e.reciprocal(wrk[:, 40:48], wrk[:, 32:40]),
                   wait=[(s_act, s_act.now())])
            p.emit(V, lambda e: e.drain())
            p.emit(V, lambda e: e.tensor_tensor(wrk[:, 48:56], wrk[:, 32:40],
                                                wrk[:, 40:48], ALU.mult))
            p.emit(V, lambda e: e.drain())
            p.emit(V, lambda e: e.tensor_tensor(wrk[:, 56:64], wrk[:, 48:56],
                                                wrk[:, 40:48], ALU.mult))
            p.emit(V, lambda e: e.drain())
            p.emit(V, lambda e, c0=c0: e.scalar_tensor_tensor(
                rstd[:, c0:c0 + 8], wrk[:, 40:48], 2.0, wrk[:, 56:64],
                ALU.mult, ALU.subtract))
            p.emit(V, lambda e: e.drain())
            p.emit(V, lambda e, c0=c0: e.scalar_tensor_tensor(
                nmrc[:, c0:c0 + 8], wrk[:, 0:8], -1.0, rstd[:, c0:c0 + 8],
                ALU.mult, ALU.mult))
            p.emit(V, lambda e: e.drain())
            for i in range(8):
                tc = 8 * g + i
                p.emit(V, lambda e, i=i, tc=tc: e.tensor_scalar(
                    xntm[:, :], ao32[:, i * 1024:(i + 1) * 1024],
                    rstd[:, tc:tc + 1], nmrc[:, tc:tc + 1], ALU.mult, ALU.add),
                    wait=[(s_tr, 2 * tc)], inc=[(s_ntm, 1)])
                p.emit(V, lambda e, tc=tc: e.reduce_max(
                    xmax[:, tc:tc + 1], xntm[:, :], AX, apply_absolute_value=True),
                    inc=[(s_xev, 1)])
                for half in range(2):
                    for q in range(4):
                        dcq = half * 4 + q
                        p.emit("tensor", lambda e, dcq=dcq, half=half, q=q: e.transpose(
                            ps_sc16[:, half * 1024 + q * 256:half * 1024 + q * 256 + 128],
                            xntm[:, dcq * 128:(dcq + 1) * 128], ident[:, :]),
                            wait=([(s_ntm, s_ntm.now()), (s_xev, s_xev.now())]
                                  if q == 0 else []),
                            inc=([(s_tr, 1)] if q == 3 else []))
                    for q in range(4):
                        dcq = half * 4 + q
                        if tc < 16:
                            tgt, toff = xn_own, dcq * 2048 + tc * 128
                        else:
                            tgt, toff = am16, XNP + dcq * 2048 + (tc - 16) * 128
                        p.emit("vector", lambda e, tgt=tgt, toff=toff, half=half, q=q:
                               e.tensor_copy(tgt[:, toff:toff + 128],
                                             ps_sc16[:, half * 1024 + q * 256:half * 1024 + q * 256 + 128]),
                               wait=[(s_tr, s_tr.now())],
                               inc=([(s_xev, 1)] if q == 3 else []))
        xn_done = s_xev.now()

        for ec in range(ECK):
            for tg in range(QG):
                bank = (ec * QG + tg) % 2
                for dc in range(DC):
                    def kmm(e, ec=ec, tg=tg, dc=dc, bank=bank):
                        if tg < 4:
                            rhs = xn_own[:, dc * 2048 + tg * 512:dc * 2048 + tg * 512 + 512]
                        else:
                            rhs = am16[:, XNP + dc * 2048 + (tg - 4) * 512:
                                       XNP + dc * 2048 + (tg - 4) * 512 + 512]
                        return e.matmul(ps_av[:, bank * 512:bank * 512 + 512],
                                        am16[:, WKA + dc * 512 + ec * 128:
                                           WKA + dc * 512 + (ec + 1) * 128],
                                        rhs, start=(dc == 0), stop=(dc == 7))
                    w = []
                    if ec == 0 and tg == 0 and dc == 0:
                        w = [(s_xev, xn_done), (s_ld, 16 * 6)]
                    elif dc == 0:
                        w = [(s_kvev, s_kvev.now() - 1)]
                    p.emit("tensor", kmm, wait=w, inc=([(s_kv, 1)] if dc == 7 else []))
                p.emit("vector", lambda e, ec=ec, tg=tg, bank=bank: e.tensor_copy(
                    am16[:, KF + ec * 4096 + tg * 512:KF + ec * 4096 + tg * 512 + 512],
                    ps_av[:, bank * 512:bank * 512 + 512]),
                    wait=[(s_kv, s_kv.now())], inc=[(s_kvev, 1)])
        for tc in range(TCH):
            bank = tc % 2
            for dc in range(DC):
                def vmm(e, tc=tc, dc=dc, bank=bank):
                    if tc < 16:
                        lhs = xn_own[:, dc * 2048 + tc * 128:dc * 2048 + (tc + 1) * 128]
                    else:
                        lhs = am16[:, XNP + dc * 2048 + (tc - 16) * 128:
                                   XNP + dc * 2048 + (tc - 15) * 128]
                    return e.matmul(ps_av[:, bank * 512:bank * 512 + 512], lhs,
                                    am16[:, WVA + dc * 512:WVA + (dc + 1) * 512],
                                    start=(dc == 0), stop=(dc == 7))
                p.emit("tensor", vmm,
                       wait=([(s_kvev, s_kvev.now() - 1)] if dc == 0 else []),
                       inc=([(s_kv, 1)] if dc == 7 else []))
            p.emit("vector", lambda e, tc=tc, bank=bank: e.tensor_copy(
                vt[:, tc * 512:(tc + 1) * 512], ps_av[:, bank * 512:bank * 512 + 512]),
                wait=[(s_kv, s_kv.now())], inc=[(s_kvev, 1)])
        kv_done = s_kvev.now()

        for ec in range(ECK):
            p.emit("vector", lambda e, ec=ec: e.tensor_tensor(
                ao[:, 0:4096], am16[:, KF + ec * 4096:KF + (ec + 1) * 4096],
                am16[:, KF + ec * 4096:KF + (ec + 1) * 4096], ALU.mult),
                wait=([(s_kvev, 32)] if ec == 0 else []), inc=[(s_kn, 1)])
            for tg in range(QG):
                p.emit("tensor", lambda e, tg=tg: e.matmul(
                    ps_b3[0:64, :], onesb[0:64, 0:64],
                    ao[0:64, tg * 512:(tg + 1) * 512],
                    start=True, stop=True, tile_position=(0, 0)),
                    wait=[(s_kn, s_kn.now()), (s_aug, s_aug.now())])
                p.emit("tensor", lambda e, tg=tg: e.matmul(
                    ps_b3[64:128, :], onesb[64:128, 64:128],
                    ao[64:128, tg * 512:(tg + 1) * 512],
                    start=True, stop=True, tile_position=(64, 64)),
                    inc=[(s_kn, 1)])
                p.emit("vector", lambda e: e.reduce_max(
                    wrk[:, 9:10], ps_b3[:, :], AX), wait=[(s_kn, s_kn.now())])
                p.emit("vector", lambda e: e.drain())
                if ec == 0 and tg == 0:
                    p.emit("vector", lambda e: e.tensor_copy(wrk[:, 8:9], wrk[:, 9:10]),
                           inc=[(s_aug, 1)])
                else:
                    p.emit("vector", lambda e: e.tensor_max(wrk[:, 8:9], wrk[:, 8:9],
                                                            wrk[:, 9:10]),
                           inc=[(s_aug, 1)])
                p.emit("vector", lambda e: e.drain())
        p.emit("gpsimd", lambda e: e.dma_start(out=lmax_sb[0:1, 0:128], in_=wrk[:, 8:9],
                                               allow_slow_non_contiguous=True),
               wait=[(s_aug, s_aug.now())], inc=[(s_cdma, 16)])
        p.emit("vector", lambda e: e.reduce_max(wrk[0:1, 10:11], lmax_sb[0:1, 0:128], AX),
               wait=[(s_cdma, s_cdma.now())])
        p.emit("vector", lambda e: e.drain(), inc=[(s_aug, 1)])
        p.emit("scalar", lambda e: e.activation(wrk[0:1, 11:12], wrk[0:1, 10:11],
                                                ACTF.Sqrt),
               wait=[(s_aug, s_aug.now())])
        p.emit("scalar", lambda e: e.drain(), inc=[(s_aug, 1)])
        p.emit("vector", lambda e: e.tensor_scalar(wrk[0:1, 12:13], wrk[0:1, 11:12],
                                                   -1.0, 0.0, ALU.mult, ALU.add),
               wait=[(s_aug, s_aug.now())])
        p.emit("vector", lambda e: e.drain(), inc=[(s_aug, 1)])
        p.emit("tensor", lambda e: e.matmul(ps_b3[:, 0:1], ones1f[0:1, :],
                                            wrk[0:1, 12:13], start=True, stop=True),
               wait=[(s_aug, s_aug.now())], inc=[(s_bc, 1)])
        p.emit("vector", lambda e: e.tensor_copy(mk_pm[:, :], ps_b3[:, 0:1]),
               wait=[(s_bc, s_bc.now())])
        p.emit("vector", lambda e: e.drain(), inc=[(s_aug, 1)])
        kn_done = s_aug.now()

        barrier()
        p.emit("vector", lambda e: e.drain(), wait=[(s_xev, xn_done)])
        p.emit("vector", lambda e: e.reduce_max(redt[:, 0:1], xmax[:, :], AX,
                                                apply_absolute_value=True))
        p.emit("vector", lambda e: e.drain(), inc=[(s_sx, 1)])
        p.emit("gpsimd", lambda e: e.memset(lmax_sb[:, :], 0.0))
        p.emit("gpsimd", lambda e: e.dma_start(out=lmax_sb[0:1, 0:128], in_=redt[:, 0:1],
                                               allow_slow_non_contiguous=True),
               wait=[(s_sx, s_sx.now())], inc=[(s_cdma, 16)])
        p.emit("vector", lambda e: e.reduce_max(lmax_sb[0:1, 128:129], lmax_sb[0:1, 0:128], AX),
               wait=[(s_cdma, s_cdma.now())])
        p.emit("vector", lambda e: e.drain(), inc=[(s_sx, 1)])
        p.emit("gpsimd", lambda e: e.dma_start(out=d_arx_i.ap(), in_=lmax_sb[0:1, 128:144]),
               wait=[(s_sx, s_sx.now())], inc=[(s_cdma, 16)])
        p.emit("gpsimd", lambda e: e.collective_compute(
            "AllReduce", ALU.max, replica_groups=[list(range(8))],
            ins=[d_arx_i.ap()], outs=[d_arx_o.ap()]),
            wait=[(s_cdma, s_cdma.now())], inc=[(s_cc, 1)])
        p.emit("gpsimd", lambda e: e.dma_start(out=gmax_sb[:, :], in_=d_arx_o.ap()),
               wait=[(s_cc, s_cc.now())], inc=[(s_cdma, 16)])
        p.emit("vector", lambda e: e.tensor_tensor(scal[0:1, 0:1], gmax_sb[0:1, 0:1],
                                                   sig_sb[0:1, 0:1], ALU.mult),
               wait=[(s_cdma, s_cdma.now()), (s_sx, 1)])
        p.emit("vector", lambda e: e.drain())
        p.emit("vector", lambda e: e.reciprocal(scal[0:1, 1:2], scal[0:1, 0:1]))
        p.emit("vector", lambda e: e.drain())
        p.emit("vector", lambda e: e.tensor_scalar(scal[0:1, 1:2], scal[0:1, 1:2],
                                                   QMAX, 0.0, ALU.mult, ALU.add))
        p.emit("vector", lambda e: e.tensor_scalar(scal[0:1, 2:3], scal[0:1, 0:1],
                                                   1.0 / QMAX, 0.0, ALU.mult, ALU.add))
        p.emit("vector", lambda e: e.drain(), inc=[(s_sx, 1)])
        return xn_done, kv_done, kn_done

    def emit_sx_bcast():
        p.emit("tensor", lambda e: e.matmul(ps_b3[:, 0:1], ones1f[0:1, :],
                                            scal[0:1, 1:2], start=True, stop=True),
               wait=[(s_sx, s_sx.now()), (s_bc, s_bc.now()),
                     (s_qev, s_qev.now()), (s_nrm, s_nrm.now())],
               inc=[(s_bc, 1)])
        p.emit("tensor", lambda e: e.matmul(ps_b3[:, 1:2], ones1f[0:1, :],
                                            scal[0:1, 2:3], start=True, stop=True),
               inc=[(s_bc, 1)])
        p.emit("vector", lambda e: e.tensor_copy(rsx_pm[:, :], ps_b3[:, 0:1]),
               wait=[(s_bc, s_bc.now())], inc=[(s_bc, 1)])
        p.emit("vector", lambda e: e.tensor_copy(sx_pm[:, :], ps_b3[:, 1:2]))
        p.emit("vector", lambda e: e.drain(), inc=[(s_bc, 1)])
        marks["sxb"] = s_bc.now()

    def emit_scale_prepass():
        jobs = ([(d_wgn, ic, rmg, DD) for ic in range(ICN)] +
                [(d_wun, ic, rmu, DD) for ic in range(ICN)] +
                [(d_wdn, hc, rmd, II) for hc in range(DC)])
        segmap = []
        col = 0
        for (w, ch, dst, wide) in jobs:
            cols = []
            pos = 0
            while pos < wide:
                seg = min(512, wide - pos)
                p.emit("gpsimd", lambda e, w=w, ch=ch, pos=pos, seg=seg: e.dma_start(
                    out=am32[:, NATF:NATF + seg],
                    in_=w[ch * 128:(ch + 1) * 128, pos:pos + seg]),
                    wait=[(s_scl, s_scl.now())], inc=[(s_nat, 16)])
                p.emit("vector", lambda e, seg=seg, col=col: e.reduce_max(
                    segred[:, col:col + 1], am32[:, NATF:NATF + seg], AX,
                    apply_absolute_value=True), wait=[(s_nat, s_nat.now())],
                    inc=[(s_scl, 1)])
                cols.append(col)
                col += 1
                pos += seg
            segmap.append((dst, ch, cols))
        for (dst, ch, cols) in segmap:
            lo, hi = cols[0], cols[-1] + 1
            p.emit("vector", lambda e, dst=dst, ch=ch, lo=lo, hi=hi: e.reduce_max(
                dst[:, ch:ch + 1], segred[:, lo:hi], AX))
        p.emit("vector", lambda e: e.drain())
        for (srct, dst, n) in ((rmg, rgc, 22), (rmu, ruc, 22), (rmd, rdc, 8)):
            p.emit("vector", lambda e, srct=srct, dst=dst, n=n: e.reciprocal(
                dst[:, 0:n], srct[:, 0:n]))
            p.emit("vector", lambda e: e.drain())
            p.emit("vector", lambda e, dst=dst, n=n: e.tensor_scalar(
                dst[:, 0:n], dst[:, 0:n], QMAX, 0.0, ALU.mult, ALU.add))
            p.emit("vector", lambda e: e.drain())
        p.emit("vector", lambda e: e.engine_nop() if hasattr(e, "engine_nop") else e.nop(),
               inc=[(s_scl, 1)])

    def emit_qgen(qg, extra_waits=()):
        buf = qg % 2
        for ec in range(ECK):
            for dc in range(DC):
                def qmm(e, qg=qg, ec=ec, dc=dc):
                    t0 = qg * 512
                    if qg < 4:
                        rhs = xn_own[:, dc * 2048 + t0:dc * 2048 + t0 + 512]
                    else:
                        rhs = am16[:, XNP + dc * 2048 + t0 - 2048:
                                   XNP + dc * 2048 + t0 - 2048 + 512]
                    return e.matmul(ps_b3[:, :],
                                    wq_sb[:, dc * 512 + ec * 128:dc * 512 + (ec + 1) * 128],
                                    rhs, start=(dc == 0), stop=(dc == 7))
                w = list(extra_waits) if (ec == 0 and dc == 0) else []
                if dc == 0:
                    w.append((s_qev, s_qev.now()))
                    w.append((s_bc, s_bc.now()))
                    if ec == 0:
                        w.append((s_nrm, s_nrm.now()))
                p.emit("tensor", qmm, wait=w, inc=([(s_qmm, 1)] if dc == 7 else []))
            p.emit("vector", lambda e, buf=buf, ec=ec: e.tensor_copy(
                am16[:, QOD + buf * 2048 + ec * 512:QOD + buf * 2048 + (ec + 1) * 512],
                ps_b3[:, :]), wait=[(s_qmm, s_qmm.now())])
            p.emit("vector", lambda e: e.drain(), inc=[(s_qev, 1)])

    def emit_xq():
        for dc in range(DC):
            for tg in range(4):
                off = dc * 2048 + tg * 512
                p.emit("vector", lambda e, off=off: e.tensor_scalar(
                    scr32[:, :], xn_own[:, off:off + 512], rsx_pm[:, 0:1], MAGIC,
                    ALU.mult, ALU.add),
                    wait=[(s_bc, marks["sxb"]), (s_qev, s_qev.now())])
                p.emit("vector", lambda e, off=off: e.tensor_scalar(
                    xn_own[:, off:off + 512], scr32[:, :], -MAGIC, QMAX,
                    ALU.add, ALU.min))
                p.emit("vector", lambda e, off=off: e.tensor_scalar_max(
                    xn_own[:, off:off + 512], xn_own[:, off:off + 512], -QMAX - 1.0),
                    inc=[(s_xq, 1)])

    def phase_B(xn_done, kv_done, kn_done):
        emit_qgen(0, [(s_kvev, kv_done)])
        qgen_emitted = 1
        xq_emitted = False
        for qg in range(QG):
            qbuf = qg % 2
            q_ready = s_qev.now()
            for grp in range(2):
                p.emit("vector", lambda e, qbuf=qbuf, grp=grp: e.tensor_tensor(
                    junkb[:, 0:1024],
                    am16[:, QOD + qbuf * 2048 + grp * 1024:QOD + qbuf * 2048 + (grp + 1) * 1024],
                    am16[:, QOD + qbuf * 2048 + grp * 1024:QOD + qbuf * 2048 + (grp + 1) * 1024],
                    ALU.mult), wait=[(s_qev, q_ready)])
                p.emit("vector", lambda e: e.drain(), inc=[(s_aug, 1)])
                for h in range(4):
                    par = h % 2
                    prq = h // 2
                    def qsqmm(e, par=par, prq=prq, h=h):
                        return e.matmul(
                            ps_sum[32 * h:32 * h + 32, :],
                            onesb[64 * par:64 * par + 64, 0:32],
                            junkb[64 * par:64 * par + 64, prq * 512:(prq + 1) * 512],
                            start=True, stop=True,
                            tile_position=(64 * par, 32 * h))
                    w = []
                    if h == 0:
                        w = [(s_aug, s_aug.now()), (s_nrm, s_nrm.now())]
                    p.emit("tensor", qsqmm, wait=w,
                           inc=([(s_summ, 1)] if h == 3 else []))
                p.emit("vector", lambda e: e.tensor_copy(scr32[:, :], ps_sum[:, :]),
                       wait=[(s_summ, s_summ.now())])
                p.emit("vector", lambda e: e.drain(), inc=[(s_aug, 1)])
                p.emit("scalar", lambda e: e.activation(junkb[:, 0:512], scr32[:, :],
                                                        ACTF.Sqrt),
                       wait=[(s_aug, s_aug.now())])
                p.emit("scalar", lambda e: e.drain(), inc=[(s_aug, 1)])
                p.emit("vector", lambda e: e.tensor_scalar(
                    am[:, QAUG:QAUG + 512], junkb[:, 0:512], mk_pm[:, 0:1], 0.0,
                    ALU.mult, ALU.add),
                    wait=[(s_aug, s_aug.now()), (s_scm, s_scm.now())])
                p.emit("vector", lambda e: e.drain(), inc=[(s_aug, 1)])
                aug_ready = s_aug.now()
                av_ckpt = []
                summ_ckpt = []
                for kc in range(KC):
                    pbuf = kc % 2
                    for pr in range(2):
                        ec = grp * 2 + pr
                        for par in range(2):
                            h = 2 * pr + par
                            def smm(e, ec=ec, par=par, kc=kc, h=h, qbuf=qbuf):
                                lo = par * 64
                                return e.matmul(
                                    ps_sc[:, h * 512:(h + 1) * 512],
                                    am16[lo:lo + 64, KF + ec * 4096 + kc * 128:
                                         KF + ec * 4096 + (kc + 1) * 128],
                                    am16[lo:lo + 64, QOD + qbuf * 2048 + ec * 512:
                                         QOD + qbuf * 2048 + (ec + 1) * 512],
                                    start=True, stop=False, tile_position=(lo, 0))
                            w = []
                            if par == 0:
                                w = [(s_exp, max(0, s_exp.now() - (1 if pr == 0 else 0))),
                                     (s_qev, q_ready)]
                                if kc == 0:
                                    w.append((s_aug, aug_ready))
                                if kc == 0 and qg == 0 and grp == 0 and pr == 0:
                                    w += [(s_kvev, kv_done), (s_xev, xn_done)]
                            p.emit("tensor", smm, wait=w)
                    for h in range(4):
                        def augmm(e, h=h):
                            return e.matmul(
                                ps_sc[:, h * 512:(h + 1) * 512],
                                onesb[32 * h:32 * h + 1, 0:128],
                                am[32 * h:32 * h + 1, QAUG:QAUG + 512],
                                start=False, stop=True, tile_position=(32 * h, 0))
                        p.emit("tensor", augmm,
                               inc=([(s_scm, 1)] if h in (1, 3) else []))
                    for half in range(2):
                        def expfn(e, pbuf=pbuf, half=half):
                            return e.activation(
                                am[:, PT + pbuf * 2048 + half * 1024:
                                   PT + pbuf * 2048 + half * 1024 + 1024],
                                ps_sc[:, half * 1024:(half + 1) * 1024], ACTF.Exp,
                                scale=pmc[:, 0:1], bias=pmc[:, 1:2])
                        ew = [(s_scm, s_scm.now() - (1 if half == 0 else 0))]
                        if kc >= 2:
                            ew.append((s_summ, summ_ckpt[kc - 2]))
                        elif qg + grp > 0:
                            ew.append((s_summ, s_summ.now()))
                        p.emit("scalar", expfn, wait=ew, inc=[(s_exp, 1)])
                    for pr in range(2):
                        for par in range(2):
                            h = 2 * pr + par
                            hd = grp * 4 + h
                            def avmm(e, pr=pr, par=par, h=h, hd=hd, kc=kc, pbuf=pbuf):
                                return e.matmul(
                                    ps_av[par * 64:(par + 1) * 64, pr * 512:pr * 512 + 512],
                                    vt[:, kc * 512 + hd * 64:kc * 512 + (hd + 1) * 64],
                                    am[:, PT + pbuf * 2048 + h * 512:
                                       PT + pbuf * 2048 + (h + 1) * 512],
                                    start=(kc == 0), stop=(kc == KC - 1),
                                    tile_position=(0, par * 64))
                            w = []
                            if par == 0:
                                w = [(s_exp, s_exp.now() - (1 if pr == 0 else 0))]
                                if kc == 0 and pr == 0:
                                    w.append((s_nrm, s_nrm.now()))
                            p.emit("tensor", avmm, wait=w,
                                   inc=([(s_avm, 1)] if (pr == 1 and par == 1) else []))
                    for h in range(4):
                        def summ(e, h=h, kc=kc, pbuf=pbuf):
                            return e.matmul(ps_sum[h * 32:(h + 1) * 32, :],
                                            onesb[:, 0:32],
                                            am[:, PT + pbuf * 2048 + h * 512:
                                               PT + pbuf * 2048 + (h + 1) * 512],
                                            start=(kc == 0), stop=(kc == KC - 1),
                                            tile_position=(0, h * 32))
                        p.emit("tensor", summ, inc=([(s_summ, 1)] if h == 3 else []))
                    av_ckpt.append(s_avm.now())
                    summ_ckpt.append(s_summ.now())
                    if grp == 0 and kc == 16 and qgen_emitted <= qg + 1 and qgen_emitted < QG:
                        emit_qgen(qgen_emitted)
                        qgen_emitted += 1
                p.emit("vector", lambda e: e.reciprocal(am32[:, RSUM32:RSUM32 + 512],
                                                        ps_sum[:, :]),
                       wait=[(s_summ, s_summ.now())], inc=[(s_nrm, 1)])
                for pr in range(2):
                    sel = selA if pr == 0 else selB
                    p.emit("tensor", lambda e, sel=sel: e.matmul(
                        ps_b3[:, :], sel[:, :], am32[:, RSUM32:RSUM32 + 512],
                        start=True, stop=True),
                        wait=[(s_nrm, s_nrm.now()), (s_qev, s_qev.now()),
                              (s_bc, s_bc.now())],
                        inc=[(s_bc, 1)])
                    p.emit("vector", lambda e, pr=pr: e.tensor_copy(
                        am[:, AOEV:AOEV + 512],
                        ps_av[:, pr * 512:(pr + 1) * 512]),
                        wait=[(s_avm, s_avm.now())])
                    def ttn(e, pr=pr, grp=grp, qg=qg):
                        ec = grp * 2 + pr
                        return e.tensor_tensor(
                            ao16[:, ec * 4096 + qg * 512:ec * 4096 + (qg + 1) * 512],
                            am[:, AOEV:AOEV + 512],
                            ps_b3[:, :], ALU.mult)
                    p.emit("vector", ttn, wait=[(s_bc, s_bc.now())], inc=[(s_nrm, 1)])
                if BARRIER_LEVEL >= 3:
                    barrier()
            if qg == 3 and not xq_emitted:
                emit_sx_bcast()
                emit_xq()
                xq_emitted = True
            if BARRIER_LEVEL >= 2:
                barrier()
        return s_nrm.now()

    def phase_OP(att_done):
        qg7 = s_qev.now()
        p.emit("gpsimd", lambda e: e.dma_start(out=am16[:, WO:WO + 8192], in_=d_wo),
               wait=[(s_qev, qg7)], inc=[(s_ld, 16)])
        wo_ld = s_ld.now()
        p.emit("gpsimd", lambda e: e.dma_start(
            out=d_ag_i.ap(), in_=bass.AP(ao16, 2048, [[16384, 128], [4096, 4], [1, 2048]])),
            wait=[(s_nrm, att_done)], inc=[(s_cdma, 16)])
        p.emit("gpsimd", lambda e: e.collective_compute(
            "AllGather", ALU.bypass, replica_groups=[[0, 1], [2, 3], [4, 5], [6, 7]],
            ins=[d_ag_i.ap()], outs=[d_ag_o.ap()]),
            wait=[(s_cdma, s_cdma.now())], inc=[(s_cc, 1)])

        def agread(e):
            with e.register("agr") as gr:
                e.reg_load(gr, psel_sb[0:1, 0:1])
                e.reg_mul(gr, gr, 128 * 8192)
                return e.dma_start(out=am16[:, AOPART:AOPART + 8192],
                                   in_=bass.AP(d_ag_o, gr, [[8192, 128], [1, 8192]]))
        p.emit("gpsimd", agread, wait=[(s_cc, s_cc.now()), (s_qev, qg7)],
               inc=[(s_cdma, 16)])
        ag_done = s_cdma.now()
        for tg in range(4):
            for dchunk in range(DC):
                bank = (tg * DC + dchunk) % 2
                for ec in range(8):
                    def opmm(e, tg=tg, dchunk=dchunk, ec=ec, bank=bank):
                        if ec < 4:
                            rhs = ao16[:, ec * 4096 + tg * 512:ec * 4096 + tg * 512 + 512]
                        else:
                            rhs = am16[:, AOPART + (ec - 4) * 2048 + tg * 512:
                                       AOPART + (ec - 4) * 2048 + tg * 512 + 512]
                        return e.matmul(ps_av[:, bank * 512:bank * 512 + 512],
                                        am16[:, WO + ec * 1024 + dchunk * 128:
                                             WO + ec * 1024 + (dchunk + 1) * 128],
                                        rhs, start=(ec == 0), stop=(ec == 7))
                    w = []
                    if ec == 0:
                        w = [(s_rep, max(0, s_rep.now() - 1))]
                        if tg == 0 and dchunk == 0:
                            w += [(s_cdma, ag_done), (s_ld, wo_ld),
                                  (s_nrm, att_done), (s_avm, s_avm.now())]
                    p.emit("tensor", opmm, wait=w, inc=([(s_op, 1)] if ec == 7 else []))
                p.emit("vector", lambda e, tg=tg, dchunk=dchunk, bank=bank: e.tensor_scalar(
                    r_r[:, dchunk * 2048 + tg * 512:dchunk * 2048 + tg * 512 + 512],
                    ps_av[:, bank * 512:bank * 512 + 512],
                    bout_f[:, dchunk:dchunk + 1], 0.0, ALU.add, ALU.add),
                    wait=[(s_op, s_op.now())], inc=[(s_rep, 1)])
        return s_rep.now()

    def phase_C1(op_done):
        p.emit("vector", lambda e: e.tensor_scalar(sgx[:, :], rmg[:, :], sx_pm[:, 0:1],
                                                   1.0 / QMAX, ALU.mult, ALU.mult),
               wait=[(s_bc, marks["sxb"]), (s_scl, s_scl.now())])
        p.emit("vector", lambda e: e.tensor_scalar(sux[:, :], rmu[:, :], sx_pm[:, 0:1],
                                                   1.0 / QMAX, ALU.mult, ALU.mult))
        p.emit("vector", lambda e: e.drain(), inc=[(s_scl, 1)])
        sgx_done = s_scl.now()
        for ic in range(ICN):
            buf = ic % 2
            p.emit("sync", lambda e, ic=ic, buf=buf: e.dma_start(
                out=ao32[:, WGST + buf * 1024:WGST + (buf + 1) * 1024],
                in_=d_wgt[ic * 128:(ic + 1) * 128, :]),
                wait=([(s_rep, op_done), (s_tap, s_tap.now())] if ic == 0 else
                      [(s_wq, max(0, s_wq.now() - 2))]),
                inc=[(s_wl, 16)])
            p.emit("sync", lambda e, ic=ic, buf=buf: e.dma_start(
                out=ao32[:, WUST + buf * 1024:WUST + (buf + 1) * 1024],
                in_=d_wut[ic * 128:(ic + 1) * 128, :]),
                inc=[(s_wl, 16)])
            p.emit("gpsimd", lambda e, ic=ic: e.dma_start(
                out=rstg[0:1, 0:128], in_=rgc[:, ic:ic + 1],
                allow_slow_non_contiguous=True),
                wait=[(s_scl, sgx_done), (s_wq, s_wq.now())], inc=[(s_nat, 16)])
            p.emit("gpsimd", lambda e, ic=ic: e.dma_start(
                out=rstg[0:1, 128:256], in_=ruc[:, ic:ic + 1],
                allow_slow_non_contiguous=True), inc=[(s_nat, 16)])
            p.emit("tensor", lambda e: e.matmul(ps_b3[:, 0:128], ones1f[0:1, :],
                                                rstg[0:1, 0:128], start=True, stop=True),
                   wait=[(s_nat, s_nat.now()), (s_wq, s_wq.now()),
                         (s_bc, s_bc.now()), (s_rep, op_done)],
                   inc=[(s_bc, 1)])
            p.emit("tensor", lambda e: e.matmul(ps_b3[:, 128:256], ones1f[0:1, :],
                                                rstg[0:1, 128:256], start=True, stop=True),
                   inc=[(s_bc, 1)])
            p.emit("vector", lambda e: e.tensor_copy(bcq[:, :], ps_b3[:, 0:256]),
                   wait=[(s_wl, s_wl.now()), (s_bc, s_bc.now())])
            p.emit("vector", lambda e: e.drain())
            for WST, WI in ((WGST, WGI), (WUST, WUI)):
                coff = 0 if WST == WGST else 128
                for hh in range(2):
                    rep = bass.AP(bcq, coff, [[256, 128], [0, 4], [1, 128]])
                    p.emit("vector", lambda e, WST=WST, buf=buf, hh=hh, rep=rep:
                           e.tensor_tensor(
                               scr32[:, :],
                               ao32[:, WST + buf * 1024 + hh * 512:
                                    WST + buf * 1024 + hh * 512 + 512],
                               rep, ALU.mult))
                    p.emit("vector", lambda e: e.tensor_scalar(
                        scr32[:, :], scr32[:, :], MAGIC, -MAGIC, ALU.add, ALU.add))
                    p.emit("vector", lambda e, WI=WI, buf=buf, hh=hh: e.tensor_scalar(
                        ao16[:, WI + buf * 1024 + hh * 512:WI + buf * 1024 + hh * 512 + 512],
                        scr32[:, :], QMAX, -QMAX - 1.0, ALU.min, ALU.max),
                        inc=([(s_wq, 1)] if hh == 1 else []))
            wq_now = s_wq.now()
            for tg in range(4):
                gbank = (ic * 4 + tg) % 2
                for j in range(DC):
                    def gmm(e, ic=ic, tg=tg, j=j, buf=buf, gbank=gbank):
                        return e.matmul(ps_sc[:, gbank * 512:gbank * 512 + 512],
                                        ao16[:, WGI + buf * 1024 + j * 128:
                                             WGI + buf * 1024 + (j + 1) * 128],
                                        xn_own[:, j * 2048 + tg * 512:
                                               j * 2048 + tg * 512 + 512],
                                        start=(j == 0), stop=(j == 7))
                    w = []
                    if j == 0:
                        w = [(s_wq, wq_now), (s_fep, max(0, s_fep.now() - 4))]
                        if ic == 0 and tg == 0:
                            w += [(s_xq, s_xq.now()), (s_exp, s_exp.now())]
                    p.emit("tensor", gmm, wait=w, inc=([(s_fmm, 1)] if j == 7 else []))
                p.emit("scalar", lambda e, ic=ic, gbank=gbank: e.activation(
                    sg_sb[:, :], ps_sc[:, gbank * 512:gbank * 512 + 512],
                    ACTF.Silu, scale=sgx[:, ic:ic + 1]),
                    wait=[(s_fmm, s_fmm.now()), (s_scl, sgx_done),
                          (s_fep, max(0, s_fep.now() - 1))], inc=[(s_fep, 1)])
                ubank = 2 + (ic * 4 + tg) % 2
                for j in range(DC):
                    def umm(e, ic=ic, tg=tg, j=j, buf=buf, ubank=ubank):
                        return e.matmul(ps_sc[:, ubank * 512:ubank * 512 + 512],
                                        ao16[:, WUI + buf * 1024 + j * 128:
                                             WUI + buf * 1024 + (j + 1) * 128],
                                        xn_own[:, j * 2048 + tg * 512:
                                               j * 2048 + tg * 512 + 512],
                                        start=(j == 0), stop=(j == 7))
                    p.emit("tensor", umm,
                           wait=([(s_fep, max(0, s_fep.now() - 4))] if j == 0 else []),
                           inc=([(s_fmm, 1)] if j == 7 else []))
                def sttm(e, ic=ic, tg=tg, ubank=ubank):
                    return e.scalar_tensor_tensor(
                        am16[:, ic * 2048 + tg * 512:ic * 2048 + (tg + 1) * 512],
                        ps_sc[:, ubank * 512:ubank * 512 + 512], sux[:, ic:ic + 1],
                        sg_sb[:, :], ALU.mult, ALU.mult)
                p.emit("vector", sttm,
                       wait=[(s_fmm, s_fmm.now()), (s_fep, s_fep.now())],
                       inc=[(s_fep, 1)])
                p.emit("vector", lambda e, ic=ic, tg=tg: e.reduce_max(
                    mwide[:, ic * 4 + tg:ic * 4 + tg + 1],
                    am16[:, ic * 2048 + tg * 512:ic * 2048 + (tg + 1) * 512],
                    AX, apply_absolute_value=True),
                    inc=[(s_fep, 1)])
        barrier()
        p.emit("vector", lambda e: e.drain(), wait=[(s_fep, s_fep.now())])
        p.emit("vector", lambda e: e.reduce_max(mwide[:, 88:89], mwide[:, 0:88], AX))
        p.emit("vector", lambda e: e.drain(), inc=[(s_fep, 1)])
        m_done = s_fep.now()
        p.emit("gpsimd", lambda e: e.memset(hmax_sb[:, :], 0.0))
        p.emit("gpsimd", lambda e: e.dma_start(out=hmax_sb[0:1, 0:128], in_=mwide[:, 88:89],
                                               allow_slow_non_contiguous=True),
               wait=[(s_fep, m_done)], inc=[(s_cdma, 16)])
        p.emit("vector", lambda e: e.reduce_max(hmax_sb[0:1, 128:129], hmax_sb[0:1, 0:128], AX),
               wait=[(s_cdma, s_cdma.now())])
        p.emit("vector", lambda e: e.drain(), inc=[(s_sh, 1)])
        p.emit("gpsimd", lambda e: e.dma_start(out=d_arh_i.ap(), in_=hmax_sb[0:1, 128:144]),
               wait=[(s_sh, s_sh.now())], inc=[(s_cdma, 16)])
        p.emit("gpsimd", lambda e: e.collective_compute(
            "AllReduce", ALU.max, replica_groups=[list(range(8))],
            ins=[d_arh_i.ap()], outs=[d_arh_o.ap()]),
            wait=[(s_cdma, s_cdma.now())], inc=[(s_cc, 1)])
        p.emit("gpsimd", lambda e: e.dma_start(out=ghmax_sb[:, :], in_=d_arh_o.ap()),
               wait=[(s_cc, s_cc.now())], inc=[(s_cdma, 16)])
        p.emit("vector", lambda e: e.tensor_tensor(scal[0:1, 4:5], ghmax_sb[0:1, 0:1],
                                                   sig_sb[0:1, 1:2], ALU.mult),
               wait=[(s_cdma, s_cdma.now())])
        p.emit("vector", lambda e: e.drain())
        p.emit("vector", lambda e: e.reciprocal(scal[0:1, 5:6], scal[0:1, 4:5]))
        p.emit("vector", lambda e: e.drain())
        p.emit("vector", lambda e: e.tensor_scalar(scal[0:1, 5:6], scal[0:1, 5:6],
                                                   QMAX, 0.0, ALU.mult, ALU.add))
        p.emit("vector", lambda e: e.tensor_scalar(scal[0:1, 6:7], scal[0:1, 4:5],
                                                   1.0 / QMAX, 0.0, ALU.mult, ALU.add))
        p.emit("vector", lambda e: e.drain(), inc=[(s_sh, 1)])
        p.emit("tensor", lambda e: e.matmul(ps_b3[:, 0:1], ones1f[0:1, :],
                                            scal[0:1, 5:6], start=True, stop=True),
               wait=[(s_sh, s_sh.now()), (s_bc, s_bc.now()), (s_fmm, s_fmm.now())],
               inc=[(s_bc, 1)])
        p.emit("tensor", lambda e: e.matmul(ps_b3[:, 1:2], ones1f[0:1, :],
                                            scal[0:1, 6:7], start=True, stop=True),
               inc=[(s_bc, 1)])
        p.emit("vector", lambda e: e.tensor_copy(rsh_pm[:, :], ps_b3[:, 0:1]),
               wait=[(s_bc, s_bc.now())], inc=[(s_bc, 1)])
        p.emit("vector", lambda e: e.tensor_copy(sh_pm[:, :], ps_b3[:, 1:2]))
        p.emit("vector", lambda e: e.drain(), inc=[(s_bc, 1)])
        marks["shb"] = s_bc.now()
        p.emit("vector", lambda e: e.tensor_scalar(sdx[:, :], rmd[:, :], sh_pm[:, 0:1],
                                                   1.0 / QMAX, ALU.mult, ALU.mult),
               wait=[(s_bc, marks["shb"])])
        p.emit("vector", lambda e: e.drain(), inc=[(s_sh, 1)])
        return m_done

    def phase_C2(m_done):
        sh_done = s_sh.now()
        for ic in range(ICN):
            for tg in range(4):
                off = ic * 2048 + tg * 512
                p.emit("vector", lambda e, off=off: e.tensor_scalar(
                    scr32[:, :], am16[:, off:off + 512], rsh_pm[:, 0:1], MAGIC,
                    ALU.mult, ALU.add),
                    wait=([(s_bc, marks["shb"]), (s_fep, m_done), (s_tap, s_tap.now())]
                          if (ic == 0 and tg == 0) else []))
                p.emit("vector", lambda e, off=off: e.tensor_scalar(
                    am16[:, off:off + 512], scr32[:, :], -MAGIC, QMAX, ALU.add, ALU.min))
                p.emit("vector", lambda e, off=off: e.tensor_scalar_max(
                    am16[:, off:off + 512], am16[:, off:off + 512], -QMAX - 1.0),
                    inc=[(s_mq, 1)])
        mq_done = s_mq.now()
        for hc in range(DC):
            buf = hc % 2
            p.emit("sync", lambda e, hc=hc, buf=buf: e.dma_start(
                out=ao32[:, buf * 2816:(buf + 1) * 2816],
                in_=d_wdt[hc * 128:(hc + 1) * 128, :]),
                wait=([(s_fep, m_done), (s_wl, s_wl.now())] if hc == 0 else
                      [(s_wq, max(0, s_wq.now() - 1))]),
                inc=[(s_wl, 16)])
            p.emit("gpsimd", lambda e, hc=hc: e.dma_start(
                out=rstgd[0:1, 0:128], in_=rdc[:, hc:hc + 1],
                allow_slow_non_contiguous=True),
                wait=[(s_wq, s_wq.now())], inc=[(s_nat, 16)])
            p.emit("tensor", lambda e: e.matmul(ps_b3[:, 0:128], ones1f[0:1, :],
                                                rstgd[0:1, 0:128], start=True, stop=True),
                   wait=[(s_nat, s_nat.now()), (s_bc, s_bc.now()),
                         (s_wq, s_wq.now()), (s_mq, mq_done)],
                   inc=[(s_bc, 1)])
            p.emit("vector", lambda e: e.tensor_copy(bcq[:, 0:128], ps_b3[:, 0:128]),
                   wait=[(s_wl, s_wl.now()), (s_bc, s_bc.now())])
            p.emit("vector", lambda e: e.drain())
            pos = 0
            while pos < II:
                seg = min(512, II - pos)
                rep = bass.AP(bcq, 0, [[256, 128], [0, seg // 128], [1, 128]])
                p.emit("vector", lambda e, buf=buf, pos=pos, seg=seg, rep=rep:
                       e.tensor_tensor(
                           scr32[:, 0:seg],
                           ao32[:, buf * 2816 + pos:buf * 2816 + pos + seg],
                           rep, ALU.mult))
                p.emit("vector", lambda e, seg=seg: e.tensor_scalar(
                    scr32[:, 0:seg], scr32[:, 0:seg], MAGIC, -MAGIC, ALU.add, ALU.add))
                p.emit("vector", lambda e, buf=buf, pos=pos, seg=seg: e.tensor_scalar(
                    arena_xn[:, WDI + buf * 2816 + pos:WDI + buf * 2816 + pos + seg],
                    scr32[:, 0:seg], QMAX, -QMAX - 1.0, ALU.min, ALU.max),
                    inc=([(s_wq, 1)] if pos + seg >= II else []))
                pos += seg
            wdq_now = s_wq.now()
            for tg in range(4):
                bank = (hc * 4 + tg) % 2
                for j in range(ICN):
                    def dmm(e, hc=hc, tg=tg, j=j, buf=buf, bank=bank):
                        return e.matmul(ps_av[:, bank * 512:bank * 512 + 512],
                                        arena_xn[:, WDI + buf * 2816 + j * 128:
                                                 WDI + buf * 2816 + (j + 1) * 128],
                                        am16[:, j * 2048 + tg * 512:j * 2048 + (tg + 1) * 512],
                                        start=(j == 0), stop=(j == ICN - 1))
                    w = []
                    if j == 0:
                        w = [(s_wq, wdq_now), (s_rep, max(0, s_rep.now() - 1))]
                        if hc == 0 and tg == 0:
                            w += [(s_mq, mq_done), (s_op, s_op.now())]
                    p.emit("tensor", dmm, wait=w,
                           inc=([(s_fmm, 1)] if j == ICN - 1 else []))
                jbuf = tg % 2
                p.emit("sync", lambda e, hc=hc, tg=tg, jbuf=jbuf: e.dma_start(
                    out=axn32[:, XJIT + jbuf * 512:XJIT + (jbuf + 1) * 512],
                    in_=d_xown[hc * 128:(hc + 1) * 128, tg * 512:(tg + 1) * 512]),
                    wait=([(s_rep, max(0, s_rep.now() - 1))] if (hc + tg) else
                          [(s_fep, m_done)]),
                    inc=[(s_jit, 16)])
                obuf = tg % 2
                p.emit("scalar", lambda e, hc=hc, bank=bank: e.activation(
                    ep_sb[:, 0:512], ps_av[:, bank * 512:bank * 512 + 512],
                    ACTF.Copy, scale=sdx[:, hc:hc + 1]),
                    wait=[(s_fmm, s_fmm.now()), (s_sh, s_sh.now()),
                          (s_rep, s_rep.now())], inc=[(s_fep, 1)])
                p.emit("vector", lambda e, hc=hc, tg=tg: e.tensor_tensor(
                    ep_sb[:, 512:1024], ep_sb[:, 0:512],
                    r_r[:, hc * 2048 + tg * 512:hc * 2048 + (tg + 1) * 512], ALU.add),
                    wait=[(s_fep, s_fep.now())])
                p.emit("vector", lambda e, jbuf=jbuf, obuf=obuf: e.tensor_tensor(
                    axn32[:, OUTSB + obuf * 512:OUTSB + (obuf + 1) * 512],
                    ep_sb[:, 512:1024],
                    axn32[:, XJIT + jbuf * 512:XJIT + (jbuf + 1) * 512], ALU.add),
                    wait=[(s_jit, s_jit.now()), (s_st, max(0, s_st.now() - 16))],
                    inc=[(s_rep, 1)])
                p.emit("sync", lambda e, hc=hc, tg=tg, obuf=obuf: e.dma_start(
                    out=d_out[hc * 128:(hc + 1) * 128, tg * 512:(tg + 1) * 512],
                    in_=axn32[:, OUTSB + obuf * 512:OUTSB + (obuf + 1) * 512]),
                    wait=[(s_rep, s_rep.now())], inc=[(s_st, 16)])

    s_jit = sem("jit")
    init_phase()
    barrier()
    xn_done, kv_done, kn_done = phase_A()
    if "xn" in taps:
        p.emit("gpsimd", lambda e: e.dma_start(out=tap_d["xn"], in_=xn_own[:, :]),
               wait=[(s_xev, xn_done)], inc=[(s_tap, 16)])
    if "xnp" in taps:
        p.emit("gpsimd", lambda e: e.dma_start(out=tap_d["xnp"], in_=am[:, XNP:XNP + 16384]),
               wait=[(s_xev, xn_done)], inc=[(s_tap, 16)])
    if "kf" in taps:
        p.emit("gpsimd", lambda e: e.dma_start(out=tap_d["kf"], in_=am[:, KF:KF + 16384]),
               wait=[(s_kvev, kv_done)], inc=[(s_tap, 16)])
    if "vt" in taps:
        p.emit("gpsimd", lambda e: e.dma_start(out=tap_d["vt"], in_=vt[:, :]),
               wait=[(s_kvev, kv_done)], inc=[(s_tap, 16)])
    if PREPASS_EARLY:
        emit_scale_prepass()
    if BARRIER_LEVEL >= 1:
        barrier()
    att_done = phase_B(xn_done, kv_done, kn_done)
    if not PREPASS_EARLY:
        emit_scale_prepass()
    if BARRIER_LEVEL >= 1:
        barrier()
    if "q0" in taps:
        p.emit("gpsimd", lambda e: e.dma_start(out=tap_d["q0"], in_=am[:, QOD:QOD + 2048]),
               wait=[(s_qev, 4)], inc=[(s_tap, 16)])
    if "ao" in taps:
        p.emit("gpsimd", lambda e: e.dma_start(out=tap_d["ao"], in_=ao[:, :]),
               wait=[(s_nrm, att_done)], inc=[(s_tap, 16)])
    if "xq" in taps:
        p.emit("gpsimd", lambda e: e.dma_start(out=tap_d["xq"], in_=xn_own[:, :]),
               wait=[(s_xq, s_xq.now())], inc=[(s_tap, 16)])
    op_done = phase_OP(att_done)
    if BARRIER_LEVEL >= 1:
        barrier()
    if "aop" in taps:
        p.emit("gpsimd", lambda e: e.dma_start(out=tap_d["aop"],
                                               in_=am[:, AOPART:AOPART + 8192]),
               wait=[(s_cdma, s_cdma.now())], inc=[(s_tap, 16)])
    if "r" in taps:
        p.emit("gpsimd", lambda e: e.dma_start(out=tap_d["r"], in_=r_r[:, :]),
               wait=[(s_rep, op_done)], inc=[(s_tap, 16)])
    m_done = phase_C1(op_done)
    if BARRIER_LEVEL >= 1:
        barrier()
    if "sc" in taps:
        def dump_sc(e):
            return e.dma_start(out=tap_d["sc"][0:6, 0:16].tensor[0:6, 0:16] if False else tap_d["sc"][0:1, 0:16], in_=lmax_sb[0:1, 0:16])
        p.emit("gpsimd", lambda e: e.dma_start(out=tap_d["sc"][0:1, 0:16], in_=lmax_sb[0:1, 0:16]),
               wait=[(s_sh, s_sh.now())], inc=[(s_tap, 16)])
        p.emit("gpsimd", lambda e: e.dma_start(out=tap_d["sc"][1:2, 0:16], in_=gmax_sb[0:1, 0:16]),
               inc=[(s_tap, 16)])
        p.emit("gpsimd", lambda e: e.dma_start(out=tap_d["sc"][2:3, 0:16], in_=hmax_sb[0:1, 0:16]),
               inc=[(s_tap, 16)])
        p.emit("gpsimd", lambda e: e.dma_start(out=tap_d["sc"][3:4, 0:16], in_=ghmax_sb[0:1, 0:16]),
               inc=[(s_tap, 16)])
        p.emit("gpsimd", lambda e: e.dma_start(out=tap_d["sc"][4:5, 0:16], in_=scal[0:1, 0:16]),
               inc=[(s_tap, 16)])
        p.emit("gpsimd", lambda e: e.dma_start(out=tap_d["sc"][5:6, 0:32], in_=xmax[0:1, 0:32]),
               inc=[(s_tap, 16)])
        p.emit("gpsimd", lambda e: e.dma_start(out=tap_d["sc"][6:7, 0:2], in_=rsx_pm[0:1, 0:1].tensor[0:1, 0:1] if False else redt[0:1, 0:2]),
               inc=[(s_tap, 16)])
    if "m" in taps:
        p.emit("gpsimd", lambda e: e.dma_start(out=tap_d["m"], in_=am[:, 0:45056]),
               wait=[(s_fep, m_done)], inc=[(s_tap, 16)])
    phase_C2(m_done)

    p.emit("sync", lambda e: e.nop(), wait=[(s_st, s_st.now())])
    if taps:
        p.emit("gpsimd", lambda e: e.nop(), wait=[(s_tap, s_tap.now())])

    p.replay()
    nc._sem_stack.close()
    return nc



def stage_inputs(inputs):
    x = np.asarray(inputs["L0_attn_x"], dtype=np.float32)
    w_qkv = np.asarray(inputs["L0_attn_w_qkv"], dtype=np.float32)
    w_out = np.asarray(inputs["L0_attn_w_out"], dtype=np.float32)
    b_out = np.asarray(inputs["L0_attn_b_out"], dtype=np.float32)
    clip_x = np.asarray(inputs["L0_ffn_clip_x"], dtype=np.float32)
    clip_down = np.asarray(inputs["L0_ffn_clip_down"], dtype=np.float32)
    w_gate = np.asarray(inputs["L0_ffn_w_gate"], dtype=np.float32)
    w_up = np.asarray(inputs["L0_ffn_w_up"], dtype=np.float32)
    w_down = np.asarray(inputs["L0_ffn_w_down"], dtype=np.float32)

    def blocked_qkv(sl):
        t = np.ascontiguousarray(sl.T)
        return np.ascontiguousarray(
            t.reshape(8, 128, 512).transpose(1, 0, 2).reshape(128, 4096))

    wg_tb = np.ascontiguousarray(
        w_gate.reshape(22, 128, 8, 128).transpose(0, 3, 2, 1).reshape(2816, 1024))
    wu_tb = np.ascontiguousarray(
        w_up.reshape(22, 128, 8, 128).transpose(0, 3, 2, 1).reshape(2816, 1024))
    wd_tb = np.ascontiguousarray(
        w_down.reshape(8, 128, 22, 128).transpose(0, 3, 2, 1).reshape(1024, 2816))
    clip2 = np.array([[clip_x[0], clip_down[0]]], dtype=np.float32)
    bpm = np.ascontiguousarray(b_out.reshape(8, 128).T).astype(np.float32)

    in_maps = []
    for c in range(8):
        b, h = c // 2, c % 2
        own = slice(h * 2048, (h + 1) * 2048)
        oth = slice((1 - h) * 2048, (2 - h) * 2048)
        xb = x[b]
        x_tm = np.ascontiguousarray(np.concatenate([xb[own], xb[oth]], axis=0))
        x_own_fm = np.ascontiguousarray(xb[own].T)
        wq = blocked_qkv(w_qkv[h * 512:(h + 1) * 512])
        wk = blocked_qkv(w_qkv[1024 + h * 512:1024 + (h + 1) * 512])
        wv = blocked_qkv(w_qkv[2048 + h * 512:2048 + (h + 1) * 512])
        colperm = np.concatenate([np.arange(h * 512, (h + 1) * 512),
                                  np.arange((1 - h) * 512, (2 - h) * 512)])
        woT = np.ascontiguousarray(w_out[:, colperm].T)
        wo_b = np.ascontiguousarray(
            woT.reshape(8, 128, 1024).transpose(1, 0, 2).reshape(128, 8192))
        in_maps.append({
            "x_tm": x_tm, "x_own_fm": x_own_fm,
            "wq_b": wq, "wk_b": wk, "wv_b": wv, "wo_b": wo_b,
            "wg_nat": w_gate, "wu_nat": w_up, "wd_nat": w_down,
            "wg_tb": wg_tb, "wu_tb": wu_tb, "wd_tb": wd_tb,
            "b_out_pm": bpm, "clip2": clip2,
            "psel": np.array([[1 - h]], dtype=np.uint32),
        })
    return in_maps


def kernel(**inputs):
    in_maps = stage_inputs(inputs)
    nc = build_program()
    res = run_bass_kernel_spmd(nc, in_maps, core_ids=list(range(8)))
    out = np.empty((4, 4096, 1024), dtype=np.float32)
    for c in range(8):
        b, h = c // 2, c % 2
        out[b, h * 2048:(h + 1) * 2048, :] = res.results[c]["out_fm"].T
    return out
